# version 17
# baseline (speedup 1.0000x reference)
import math
from contextlib import ExitStack

import numpy as np

import concourse.bass as bass
import concourse.mybir as mybir
from concourse.bass_utils import run_bass_kernel_spmd

F32 = mybir.dt.float32
F16 = mybir.dt.float16
I32 = mybir.dt.int32
AF = mybir.ActivationFunctionType
ALU = mybir.AluOpType
AX = mybir.AxisListType

HD = 128
WIN = 512
NMETA = 16
KW = 2 * WIN + NMETA
NEG = -30000.0
ISCALE = 1.0 / math.sqrt(HD)
HSC = 0.125
EPS = 1e-5
PAIRSWAP = [i ^ 1 for i in range(32)]


class Cfg:
    def __init__(self, E=2048, NH=16, NKV=4, T=2048, I=8192, n_cores=8, S=8192, B=2):
        self.E, self.NH, self.NKV, self.T, self.I = E, NH, NKV, T, I
        self.n_cores, self.S, self.B = n_cores, S, B
        self.DC = E // 128
        self.NCH = T // WIN
        self.WKT = T + WIN
        self.KVT = self.WKT + NMETA
        self.IC = I // 128
        self.IGRP = min(8, self.IC)
        self.NGRP = self.IC // self.IGRP
        self.KVO = 2 * NKV * HD
        self.NVB = self.WKT // 128
        self.GQ = NH // NKV
        self.NKTW = self.WKT // WIN


FULL = Cfg()


class Tr:

    def __init__(self, nc, ctx):
        self.nc = nc
        self.ctx = ctx
        self.sems, self.counts, self.guar = {}, {}, {}
        self.writer, self.readers = {}, {}
        self.dma_sem_of_buf = {}
        self.n_dma_sems = 0
        self.pending_reads = {e: [] for e in ("T", "V", "S", "G", "Y")}
        for name in ("T", "V", "S", "G", "Y"):
            self.sems[name] = ctx.enter_context(nc.semaphore(f"sem{name}"))
            self.counts[name] = 0

    def eng(self, name):
        return {"T": self.nc.tensor, "V": self.nc.vector, "S": self.nc.scalar,
                "G": self.nc.gpsimd, "Y": self.nc.sync}[name]

    def _dma_sem(self, buf):
        name = self.dma_sem_of_buf.get(id(buf))
        if name is None:
            name = f"D{self.n_dma_sems}"
            self.n_dma_sems += 1
            self.sems[name] = self.ctx.enter_context(self.nc.semaphore(f"sem{name}"))
            self.counts[name] = 0
            self.dma_sem_of_buf[id(buf)] = name
        return name

    def _wait(self, eng_name, sem_name, val):
        if val <= 0:
            return
        if eng_name == "T" and sem_name == "T":
            return
        g = self.guar.setdefault(eng_name, {})
        if g.get(sem_name, 0) >= val:
            return
        self.eng(eng_name).wait_ge(self.sems[sem_name], val)
        g[sem_name] = val

    def op(self, eng_name, fn, reads=(), writes=(), dma=False, inc=True):
        for b in list(reads) + list(writes):
            w = self.writer.get(id(b))
            if w is not None:
                self._wait(eng_name, w[0], w[1])
        for b in writes:
            for (sm, vl) in self.readers.get(id(b), ()):
                self._wait(eng_name, sm, vl)
        instr = fn()
        if dma:
            assert len(writes) == 1
            sem_name = self._dma_sem(writes[0])
            self.counts[sem_name] += 16
            instr.then_inc(self.sems[sem_name], 16)
        else:
            sem_name = eng_name
            self.counts[sem_name] += 1
            instr.then_inc(self.sems[sem_name], 1)
        tok = (sem_name, self.counts[sem_name])
        for b in reads:
            self.readers.setdefault(id(b), []).append(tok)
        for b in writes:
            self.writer[id(b)] = tok
            self.readers[id(b)] = []
        return instr

    def barrier(self):
        for e in ("T", "V", "S", "G", "Y"):
            for sname, cval in list(self.counts.items()):
                if cval > 0:
                    self._wait(e, sname, cval)

    def finish(self, eng_name="Y"):
        for sname, cval in list(self.counts.items()):
            if cval > 0:
                self._wait(eng_name, sname, cval)


class Bank:
    def __init__(self, ap32):
        self.ap32 = ap32


def build(cfg):
    nc = bass.Bass("TRN2", target_bir_lowering=False)
    c = cfg

    xk = nc.dram_tensor("xk", [c.KVT, c.E], F32, kind="ExternalInput").ap()
    wq = nc.dram_tensor("wq", [c.E, c.E], F32, kind="ExternalInput").ap()
    wkv = nc.dram_tensor("wkv", [c.E, c.KVO], F32, kind="ExternalInput").ap()
    wo = nc.dram_tensor("wo", [c.E, c.E], F32, kind="ExternalInput").ap()
    gw = nc.dram_tensor("gw", [c.I, c.E], I32, kind="ExternalInput").ap()
    uw = nc.dram_tensor("uw", [c.I, c.E], I32, kind="ExternalInput").ap()
    dw = nc.dram_tensor("dw", [c.E, c.I], I32, kind="ExternalInput").ap()
    bq_in = nc.dram_tensor("bq_t", [128, c.DC], F32, kind="ExternalInput").ap()
    bkvk_in = nc.dram_tensor("bkv_k", [128, c.NKV], F32, kind="ExternalInput").ap()
    bkvv_in = nc.dram_tensor("bkv_vb", [128, c.NKV * HD], F32, kind="ExternalInput").ap()
    bob_in = nc.dram_tensor("bo_b", [128, c.E], F32, kind="ExternalInput").ap()
    gsc_in = nc.dram_tensor("gsc_t", [128, c.IC], F32, kind="ExternalInput").ap()
    usc_in = nc.dram_tensor("usc_t", [128, c.IC], F32, kind="ExternalInput").ap()
    dsc_in = nc.dram_tensor("dsc_t", [128, c.DC], F32, kind="ExternalInput").ap()
    cos_in = nc.dram_tensor("cos_t", [128, c.KVT], F32, kind="ExternalInput").ap()
    sin_in = nc.dram_tensor("sin_t", [128, c.KVT], F32, kind="ExternalInput").ap()
    mask_in = nc.dram_tensor("mask_t", [c.NCH, WIN, KW], F16, kind="ExternalInput").ap()
    idf_in = nc.dram_tensor("identf", [128, 128], F32, kind="ExternalInput").ap()
    idh_in = nc.dram_tensor("identh", [128, 128], F16, kind="ExternalInput").ap()
    y = nc.dram_tensor("y", [c.T, c.E], F32, kind="ExternalOutput").ap()

    qs_h = nc.dram_tensor("qs_h", [c.NH, 128, c.T], F16).ap()
    qs_l = nc.dram_tensor("qs_l", [c.NH, 128, c.T], F16).ap()
    ks_h = nc.dram_tensor("ks_h", [c.NKV, 128, c.KVT], F16).ap()
    ks_l = nc.dram_tensor("ks_l", [c.NKV, 128, c.KVT], F16).ap()
    vs_h = nc.dram_tensor("vs_h", [c.NVB + 1, 128, c.NKV * HD], F16).ap()
    vs_l = nc.dram_tensor("vs_l", [c.NVB + 1, 128, c.NKV * HD], F16).ap()
    x1s = nc.dram_tensor("x1s", [c.T, c.E], F32).ap()
    wqs_h = nc.dram_tensor("wqs_h", [c.DC, 128, c.E], F16).ap()
    wqs_l = nc.dram_tensor("wqs_l", [c.DC, 128, c.E], F16).ap()
    wkvs_h = nc.dram_tensor("wkvs_h", [c.DC, 128, c.KVO], F16).ap()
    wkvs_l = nc.dram_tensor("wkvs_l", [c.DC, 128, c.KVO], F16).ap()
    wos_h = nc.dram_tensor("wos_h", [c.DC, 128, c.E], F16).ap()
    wos_l = nc.dram_tensor("wos_l", [c.DC, 128, c.E], F16).ap()
    wgs = nc.dram_tensor("wgs", [c.IC, 128, c.E], F16).ap()
    wus = nc.dram_tensor("wus", [c.IC, 128, c.E], F16).ap()
    wds = nc.dram_tensor("wds", [c.NGRP, c.DC, 128, c.IGRP * 128], F16).ap()

    with ExitStack() as top:
        def sb(stk, n, cols, dt=F32, parts=128):
            return stk.enter_context(nc.sbuf_tensor(n, [parts, cols], dt))[:]

        psA = top.enter_context(nc.psum_tensor("psA", [128, 1536], F32))[:]
        psB = top.enter_context(nc.psum_tensor("psB", [128, 1536], F32))[:]
        psC = top.enter_context(nc.psum_tensor("psC", [128, 512], F32))[:]
        banks = [Bank(psA[:, j * 512:(j + 1) * 512]) for j in range(3)] + \
                [Bank(psB[:, j * 512:(j + 1) * 512]) for j in range(3)] + \
                [Bank(psC)]

        identf = sb(top, "c_identf", 128)
        identh = sb(top, "c_identh", 128, F16)
        bqt = sb(top, "bqt", c.DC)
        bkvk = sb(top, "bkvk", c.NKV)
        bkvv = sb(top, "bkvv", c.NKV * HD)
        bob = sb(top, "bob", c.E)
        gsc = sb(top, "gsc", c.IC)
        usc = sb(top, "usc", c.IC)
        dsc8 = sb(top, "dsc8", c.DC)
        dsc_raw = sb(top, "dsc_raw", c.DC)
        negc = sb(top, "negc", 1)

        with nc.Block():
            tr = Tr(nc, top)
            v, s, g, t = nc.vector, nc.scalar, nc.gpsimd, nc.tensor

            def ld(dst, src, buf, reads=(), eng="Y", slow=False):
                kw = {"allow_slow_non_contiguous": True} if slow else {}
                tr.op(eng, lambda: tr.eng(eng).dma_start(out=dst, in_=src, **kw),
                      reads=reads, writes=[buf], dma=True)

            def stc(dst, src, dstbuf, srcbuf, eng="G"):
                tr.op(eng, lambda: tr.eng(eng).dma_start(out=dst, in_=src),
                      reads=[srcbuf], writes=[dstbuf], dma=True)

            for dst, src in ((identf, idf_in), (identh, idh_in), (bqt, bq_in),
                             (bkvk, bkvk_in), (bkvv, bkvv_in), (bob, bob_in),
                             (gsc, gsc_in), (usc, usc_in), (dsc_raw, dsc_in)):
                ld(dst, src, dst)
            tr.op("V", lambda: v.tensor_scalar(dsc8, dsc_raw, 1.0 / HSC, None, ALU.mult),
                  reads=[dsc_raw], writes=[dsc8])
            tr.op("V", lambda: v.memset(negc, -20.0), writes=[negc])

            def split(src, hi, lo, rbufs, hbuf, lbuf, scale=None,
                      eng_hi="G", eng_lo="G"):
                rb = list(rbufs)
                if eng_hi == "S":
                    tr.op("S", lambda: s.activation(hi, src, AF.Copy,
                                                    scale=(1.0 if scale is None else scale)),
                          reads=rb, writes=[hbuf])
                elif scale is None:
                    tr.op(eng_hi, lambda: tr.eng(eng_hi).tensor_copy(hi, src),
                          reads=rb, writes=[hbuf])
                else:
                    tr.op(eng_hi, lambda: tr.eng(eng_hi).tensor_scalar(
                        hi, src, scale, None, ALU.mult), reads=rb, writes=[hbuf])
                if scale is None:
                    tr.op(eng_lo, lambda: tr.eng(eng_lo).tensor_sub(lo, src, hi),
                          reads=rb + [hbuf], writes=[lbuf])
                else:
                    el = "V" if eng_lo == "G" else eng_lo
                    tr.op(el, lambda: tr.eng(el).scalar_tensor_tensor(
                        lo, src, scale, hi, ALU.mult, ALU.subtract),
                        reads=rb + [hbuf], writes=[lbuf])

            def tp_split(src, srcbuf, blocks, dsth, dstl, hbuf, lbuf, p=128,
                         bank_pick=None):
                for i0 in range(0, len(blocks), 4):
                    grp = blocks[i0:i0 + 4]
                    bank = bank_pick((i0 // 4) % 2)
                    for j, (c0, _) in enumerate(grp):
                        tr.op("T", lambda c0=c0, j=j, bank=bank: t.transpose(
                            bank.ap32[:, j * 128: j * 128 + p],
                            src[0:p, c0:c0 + 128], identf[0:p, 0:p]),
                            reads=[srcbuf, identf], writes=[bank],
                            inc=(j == len(grp) - 1))
                    for j, (_, dfn) in enumerate(grp):
                        dh = dfn(dsth)
                        dl = dfn(dstl)
                        tr.op("S", lambda dh=dh, j=j, bank=bank: s.activation(
                            dh, bank.ap32[:, j * 128: j * 128 + p], AF.Copy),
                            reads=[bank], writes=[hbuf])
                        tr.op("V", lambda dh=dh, dl=dl, j=j, bank=bank: v.tensor_sub(
                            dl, bank.ap32[:, j * 128: j * 128 + p], dh),
                            reads=[bank, hbuf], writes=[lbuf])

            with ExitStack() as ph:
                w32 = [sb(ph, f"pw32_{i}", c.E) for i in range(2)]
                wsh = [sb(ph, f"pwsh{i}", c.E, F16) for i in range(2)]
                wsl = [sb(ph, f"pwsl{i}", c.E, F16) for i in range(2)]
                for d in range(c.DC):
                    for wi_, (src, dsth, dstl, cols) in enumerate(
                            ((wq, wqs_h, wqs_l, c.E),
                             (wkv, wkvs_h, wkvs_l, c.KVO),
                             (wo, wos_h, wos_l, c.E))):
                        bsel = (d * 3 + wi_) % 2
                        wb_, hb_, lb_ = w32[bsel], wsh[bsel], wsl[bsel]
                        ld(wb_[:, 0:cols], src[d * 128:(d + 1) * 128, :], wb_)
                        split(wb_[:, 0:cols], hb_[:, 0:cols], lb_[:, 0:cols],
                              [wb_], hb_, lb_, eng_hi="S", eng_lo="V")
                        stc(dsth[d], hb_[:, 0:cols], dsth, hb_)
                        stc(dstl[d], lb_[:, 0:cols], dstl, lb_)
            tr.barrier()

            with ExitStack() as ph:
                psD = ph.enter_context(nc.psum_tensor("psD", [128, 512], F32))[:]
                bankD = Bank(psD)
                cost = sb(ph, "cost", c.KVT)
                ssint = sb(ph, "ssint", c.KVT)
                ld(cost, cos_in, cost)
                ld(ssint, sin_in, ssint)
                xtm = [sb(ph, f"xtm{i}", c.E) for i in range(4)]
                xTh = sb(ph, "xTh", c.DC * 512, F16)
                xTl = sb(ph, "xTl", c.DC * 512, F16)
                wkvh = [sb(ph, f"wkvh{i}", c.KVO, F16) for i in range(4)]
                wkvl = [sb(ph, f"wkvl{i}", c.KVO, F16) for i in range(4)]
                wqh = [sb(ph, f"wqh{i}", 1024, F16) for i in range(6)]
                wql = [sb(ph, f"wql{i}", 1024, F16) for i in range(6)]
                ro = [sb(ph, f"ro{i}", 512) for i in range(9)]
                sp16 = [sb(ph, f"sp16_{i}", 512, F16) for i in range(8)]

                kbanks = [banks[0], banks[1], banks[2], banks[3]][:c.NKV]
                vbanks = [banks[4], banks[5], banks[6], bankD]

                ktiles = [(kt * WIN, WIN) for kt in range(c.NKTW)] + [(c.WKT, NMETA)]

                def emit_x_loads(ti):
                    slot0_, w_ = ktiles[ti]
                    for j in range((w_ + 127) // 128):
                        p_ = min(128, w_ - j * 128)
                        ld(xtm[j][0:p_, :],
                           xk[slot0_ + j * 128: slot0_ + j * 128 + p_, :], xtm[j])

                emit_x_loads(0)
                for ti, (slot0, w) in enumerate(ktiles):
                    nts = (w + 127) // 128
                    for j in range(nts):
                        p = min(128, w - j * 128)
                        xb_ = xtm[j]
                        blocks = []
                        for dc in range(c.DC):
                            def dfn(buf16, dc=dc, j=j, p=p):
                                return buf16[:, dc * 512 + j * 128: dc * 512 + j * 128 + p]
                            blocks.append((dc * 128, dfn))
                        tp_split(xb_, xb_, blocks, xTh, xTl, xTh, xTl, p=p,
                                 bank_pick=lambda i: [banks[0], banks[1]][i])
                    if ti + 1 < len(ktiles):
                        emit_x_loads(ti + 1)

                    nvs = nts
                    for d in range(c.DC):
                        wb = d % 4
                        ld(wkvh[wb], wkvs_h[d], wkvh[wb])
                        ld(wkvl[wb], wkvs_l[d], wkvl[wb])
                        xmv_h = xTh[:, d * 512: d * 512 + w]
                        xmv_l = xTl[:, d * 512: d * 512 + w]
                        for kv in range(c.NKV):
                            sth = wkvh[wb][:, kv * 128:(kv + 1) * 128]
                            stl = wkvl[wb][:, kv * 128:(kv + 1) * 128]
                            bank = kbanks[kv]
                            grp = [(sth, xmv_h, [wkvh[wb], xTh]),
                                   (sth, xmv_l, [wkvh[wb], xTl]),
                                   (stl, xmv_h, [wkvl[wb], xTh])]
                            for gi, (lh, rh, rds) in enumerate(grp):
                                tr.op("T", lambda lh=lh, rh=rh, bank=bank, w=w, gi=gi, d=d:
                                      t.matmul(bank.ap32[:, 0:w], lh, rh,
                                               start=(d == 0 and gi == 0),
                                               stop=(d == c.DC - 1 and gi == 2)),
                                      reads=rds, writes=[bank],
                                      inc=(d == c.DC - 1 and gi == 2))
                        mvvh = wkvh[wb][:, c.NKV * 128: 2 * c.NKV * 128]
                        mvvl = wkvl[wb][:, c.NKV * 128: 2 * c.NKV * 128]
                        for ks_ in range(nvs):
                            p = min(128, w - ks_ * 128)
                            sth = xTh[:, d * 512 + ks_ * 128: d * 512 + ks_ * 128 + p]
                            stl = xTl[:, d * 512 + ks_ * 128: d * 512 + ks_ * 128 + p]
                            bank = vbanks[ks_]
                            grp = [(sth, mvvh, [xTh, wkvh[wb]]),
                                   (sth, mvvl, [xTh, wkvl[wb]]),
                                   (stl, mvvh, [xTl, wkvh[wb]])]
                            for gi, (lh, rh, rds) in enumerate(grp):
                                tr.op("T", lambda lh=lh, rh=rh, bank=bank, p=p, gi=gi, d=d:
                                      t.matmul(bank.ap32[0:p, 0:c.NKV * HD], lh, rh,
                                               start=(d == 0 and gi == 0),
                                               stop=(d == c.DC - 1 and gi == 2)),
                                      reads=rds, writes=[bank],
                                      inc=(d == c.DC - 1 and gi == 2))

                    for kv in range(c.NKV):
                        bank = kbanks[kv]
                        kraw, ksf, t1, t2 = ro[0], ro[1], ro[2], ro[3]
                        tr.op("S", lambda bank=bank, kv=kv, w=w: s.activation(
                            kraw[:, 0:w], bank.ap32[:, 0:w], AF.Identity,
                            bias=bkvk[:, kv:kv + 1]),
                            reads=[bank, bkvk], writes=[kraw])
                        tr.op("V", lambda w=w: v.stream_shuffle(
                            ksf[:, 0:w], kraw[:, 0:w], PAIRSWAP),
                            reads=[kraw], writes=[ksf])
                        tr.op("V", lambda w=w, slot0=slot0: v.tensor_mul(
                            t1[:, 0:w], kraw[:, 0:w], cost[:, slot0:slot0 + w]),
                            reads=[kraw, cost], writes=[t1])
                        tr.op("V", lambda w=w, slot0=slot0: v.tensor_mul(
                            t2[:, 0:w], ksf[:, 0:w], ssint[:, slot0:slot0 + w]),
                            reads=[ksf, ssint], writes=[t2])
                        tr.op("V", lambda w=w: v.tensor_add(
                            t1[:, 0:w], t1[:, 0:w], t2[:, 0:w]),
                            reads=[t2, t1], writes=[t1])
                        kh16, kl16 = sp16[0], sp16[1]
                        split(t1[:, 0:w], kh16[:, 0:w], kl16[:, 0:w], [t1], kh16, kl16)
                        stc(ks_h[kv, :, slot0:slot0 + w], kh16[:, 0:w], ks_h, kh16)
                        stc(ks_l[kv, :, slot0:slot0 + w], kl16[:, 0:w], ks_l, kl16)

                    for ks_ in range(nvs):
                        p = min(128, w - ks_ * 128)
                        bank = vbanks[ks_]
                        vtmp = ro[4]
                        tr.op("V", lambda bank=bank, p=p: v.tensor_add(
                            vtmp[0:p, 0:c.NKV * HD], bank.ap32[0:p, 0:c.NKV * HD],
                            bkvv[0:p, :]),
                            reads=[bank, bkvv], writes=[vtmp])
                        vh16, vl16 = sp16[2], sp16[3]
                        split(vtmp[0:p, 0:c.NKV * HD], vh16[0:p, 0:c.NKV * HD],
                              vl16[0:p, 0:c.NKV * HD], [vtmp], vh16, vl16,
                              eng_hi="S", eng_lo="V")
                        blk = c.NVB if w == NMETA else (slot0 // 128 + ks_)
                        stc(vs_h[blk, 0:p, :], vh16[0:p, 0:c.NKV * HD], vs_h, vh16)
                        stc(vs_l[blk, 0:p, :], vl16[0:p, 0:c.NKV * HD], vs_l, vl16)

                    if slot0 >= WIN and w == WIN:
                        tok0 = slot0 - WIN
                        for half in range(0, c.DC, 8):
                            nh_ = min(8, c.DC - half)
                            qbanks = (banks + [bankD])[:nh_] if nh_ <= 7 else \
                                banks[:7] + [bankD]
                            for d in range(c.DC):
                                qb = d % 6
                                ld(wqh[qb][:, 0:nh_ * 128],
                                   wqs_h[d, :, half * 128:(half + nh_) * 128], wqh[qb])
                                ld(wql[qb][:, 0:nh_ * 128],
                                   wqs_l[d, :, half * 128:(half + nh_) * 128], wql[qb])
                                xmv_h = xTh[:, d * 512: d * 512 + w]
                                xmv_l = xTl[:, d * 512: d * 512 + w]
                                for dq in range(nh_):
                                    sth = wqh[qb][:, dq * 128:(dq + 1) * 128]
                                    stl = wql[qb][:, dq * 128:(dq + 1) * 128]
                                    bank = qbanks[dq]
                                    grp = [(sth, xmv_h, [wqh[qb], xTh]),
                                           (sth, xmv_l, [wqh[qb], xTl]),
                                           (stl, xmv_h, [wql[qb], xTh])]
                                    for gi, (lh, rh, rds) in enumerate(grp):
                                        tr.op("T", lambda lh=lh, rh=rh, bank=bank,
                                              gi=gi, d=d, w=w:
                                              t.matmul(bank.ap32[:, 0:w], lh, rh,
                                                       start=(d == 0 and gi == 0),
                                                       stop=(d == c.DC - 1 and gi == 2)),
                                              reads=rds, writes=[bank],
                                              inc=(d == c.DC - 1 and gi == 2))
                            for dq in range(nh_):
                                dout = half + dq
                                bank = qbanks[dq]
                                qraw, qsf, t1, t2 = ro[5], ro[6], ro[7], ro[8]
                                tr.op("S", lambda bank=bank, dout=dout, w=w: s.activation(
                                    qraw[:, 0:w], bank.ap32[:, 0:w], AF.Identity,
                                    bias=bqt[:, dout:dout + 1]),
                                    reads=[bank, bqt], writes=[qraw])
                                tr.op("V", lambda w=w: v.stream_shuffle(
                                    qsf[:, 0:w], qraw[:, 0:w], PAIRSWAP),
                                    reads=[qraw], writes=[qsf])
                                tr.op("V", lambda w=w, slot0=slot0: v.tensor_mul(
                                    t1[:, 0:w], qraw[:, 0:w], cost[:, slot0:slot0 + w]),
                                    reads=[qraw, cost], writes=[t1])
                                tr.op("V", lambda w=w, slot0=slot0: v.tensor_mul(
                                    t2[:, 0:w], qsf[:, 0:w], ssint[:, slot0:slot0 + w]),
                                    reads=[qsf, ssint], writes=[t2])
                                tr.op("V", lambda w=w: v.tensor_add(
                                    t1[:, 0:w], t1[:, 0:w], t2[:, 0:w]),
                                    reads=[t2, t1], writes=[t1])
                                qh16, ql16 = sp16[4], sp16[5]
                                split(t1[:, 0:w], qh16[:, 0:w], ql16[:, 0:w], [t1],
                                      qh16, ql16, scale=ISCALE, eng_hi="S", eng_lo="V")
                                stc(qs_h[dout, :, tok0:tok0 + w], qh16[:, 0:w],
                                    qs_h, qh16)
                                stc(qs_l[dout, :, tok0:tok0 + w], ql16[:, 0:w],
                                    qs_l, ql16)
                tr.barrier()

            with ExitStack() as ph:
                psD = ph.enter_context(nc.psum_tensor("psD2", [128, 512], F32))[:]
                bankD = Bank(psD)
                qhh = [sb(ph, f"qhh{i}", 512, F16) for i in range(2)]
                qhl = [sb(ph, f"qhl{i}", 512, F16) for i in range(2)]
                kvh = [sb(ph, f"kvh{i}", KW, F16) for i in range(2)]
                kvl = [sb(ph, f"kvl{i}", KW, F16) for i in range(2)]
                VW = c.NKV * HD
                vch = sb(ph, "vch", 9 * VW, F16)
                vcl = sb(ph, "vcl", 9 * VW, F16)
                maskc = sb(ph, "maskc", 4 * KW, F16)
                Sm = [sb(ph, f"Sm{i}", KW) for i in range(2)]
                Aex = [sb(ph, f"Aex{i}", KW) for i in range(2)]
                An = [sb(ph, f"An{i}", KW) for i in range(2)]
                nmax = [sb(ph, f"nmax{i}", 1) for i in range(2)]
                ssum = [sb(ph, f"ssum{i}", 1) for i in range(2)]
                rvec = [sb(ph, f"rvec{i}", 1) for i in range(2)]
                ATh = sb(ph, "ATh", 9 * 512, F16)
                ATl = sb(ph, "ATl", 9 * 512, F16)
                Yh = sb(ph, "Yh", c.NH * 512, F16)
                Yl = sb(ph, "Yl", c.NH * 512, F16)
                woh = [sb(ph, f"woh{i}", 512, F16) for i in range(4)]
                wol = [sb(ph, f"wol{i}", 512, F16) for i in range(4)]
                xbt = [sb(ph, f"xbt{i}", c.E) for i in range(2)]
                x1pre = [sb(ph, f"x1pre{i}", c.E) for i in range(4)]
                lnsc = sb(ph, "lnsc", c.E)
                lndump = sb(ph, "lndump", c.E, F16)
                lnv = [sb(ph, f"lnv{i}", 1) for i in range(5)]

                NQT = WIN // 128
                ATr = ATh.rearrange("p (kc q) -> p kc q", kc=9)
                ATlr = ATl.rearrange("p (kc q) -> p kc q", kc=9)

                def emit_scores(ch, h, qt):
                    kv = h // c.GQ
                    kb, lb = kvh[kv % 2], kvl[kv % 2]
                    qb, qlb = qhh[h % 2], qhl[h % 2]
                    sc3 = 0 if qt % 2 == 0 else 3
                    ps_sc = psA if sc3 == 0 else psB
                    sth = qb[:, qt * 128:(qt + 1) * 128]
                    stl = qlb[:, qt * 128:(qt + 1) * 128]
                    for si, (k0, kw_) in enumerate(((0, 512), (512, 512), (1024, NMETA))):
                        mvh = kb[:, k0:k0 + kw_]
                        mvl = lb[:, k0:k0 + kw_]
                        bank = banks[sc3 + si]
                        grp = [(sth, mvh, [qb, kb]), (sth, mvl, [qb, lb]),
                               (stl, mvh, [qlb, kb])]
                        for gi, (lh, rh, rds) in enumerate(grp):
                            tr.op("T", lambda lh=lh, rh=rh, k0=k0, kw_=kw_, gi=gi,
                                  ps_sc=ps_sc: t.matmul(
                                      ps_sc[:, k0:k0 + kw_], lh, rh,
                                      start=(gi == 0), stop=(gi == 2)),
                                  reads=rds, writes=[bank], inc=(gi == 2))

                def emit_softmax(ch, h, qt):
                    i2 = qt % 2
                    sc3 = 0 if qt % 2 == 0 else 3
                    ps_sc = psA if sc3 == 0 else psB
                    trip = [banks[sc3], banks[sc3 + 1], banks[sc3 + 2]]
                    tr.op("V", lambda: v.tensor_add(
                        Sm[i2][:, 0:KW], ps_sc[:, 0:KW],
                        maskc[:, qt * KW:(qt + 1) * KW]),
                        reads=trip + [maskc], writes=[Sm[i2]])
                    tr.op("S", lambda: s.activation(Aex[i2][:, 0:KW], Sm[i2][:, 0:KW],
                                                    AF.Exp, bias=negc,
                                                    accum_out=ssum[i2]),
                          reads=[Sm[i2], negc], writes=[Aex[i2], ssum[i2]])
                    tr.op("V", lambda: v.reciprocal(rvec[i2], ssum[i2]),
                          reads=[ssum[i2]], writes=[rvec[i2]])
                    Ann = An[i2]
                    tr.op("V", lambda Ann=Ann: v.tensor_scalar(
                        Ann[:, 0:KW], Aex[i2][:, 0:KW], rvec[i2], None, ALU.mult),
                        reads=[Aex[i2], rvec[i2]], writes=[Ann])
                    for bi in range(2):
                        bank = trip[bi]
                        for j in range(4):
                            kc = bi * 4 + j
                            tr.op("T", lambda kc=kc, j=j, bank=bank: t.transpose(
                                bank.ap32[:, j * 128:(j + 1) * 128],
                                Ann[:, kc * 128:(kc + 1) * 128], identf),
                                reads=[Ann, identf], writes=[bank], inc=(j == 3))
                        dh = ATr[:, bi * 4:(bi + 1) * 4, qt * 128:(qt + 1) * 128]
                        dl = ATlr[:, bi * 4:(bi + 1) * 4, qt * 128:(qt + 1) * 128]
                        br = bank.ap32.rearrange("p (j q) -> p j q", j=4)
                        tr.op("S", lambda dh=dh, br=br, bank=bank: s.activation(
                            dh, br, AF.Copy), reads=[bank], writes=[ATh])
                        tr.op("V", lambda dh=dh, dl=dl, br=br, bank=bank: v.tensor_sub(
                            dl, br, dh), reads=[bank, ATh], writes=[ATl])
                    bank = trip[2]
                    tr.op("T", lambda bank=bank: t.transpose(
                        bank.ap32[0:NMETA, 0:128], Ann[:, 1024:1024 + NMETA],
                        identf), reads=[Ann, identf], writes=[bank])
                    dh = ATr[0:NMETA, 8:9, qt * 128:(qt + 1) * 128].squeeze(1)
                    dl = ATlr[0:NMETA, 8:9, qt * 128:(qt + 1) * 128].squeeze(1)
                    tr.op("S", lambda dh=dh, bank=bank: s.activation(
                        dh, bank.ap32[0:NMETA, 0:128], AF.Copy),
                        reads=[bank], writes=[ATh])
                    tr.op("V", lambda dh=dh, dl=dl, bank=bank: v.tensor_sub(
                        dl, bank.ap32[0:NMETA, 0:128], dh),
                        reads=[bank, ATh], writes=[ATl])

                def emit_av(ch, h):
                    kv = h // c.GQ
                    bank = banks[6] if h % 2 == 0 else bankD
                    for kc in range(9):
                        p = NMETA if kc == 8 else 128
                        sth = vch[0:p, kc * VW + kv * 128: kc * VW + (kv + 1) * 128]
                        stl = vcl[0:p, kc * VW + kv * 128: kc * VW + (kv + 1) * 128]
                        mvh = ATh[0:p, kc * 512:(kc + 1) * 512]
                        mvl = ATl[0:p, kc * 512:(kc + 1) * 512]
                        grp = [(sth, mvh, [vch, ATh]), (sth, mvl, [vch, ATl]),
                               (stl, mvh, [vcl, ATh])]
                        for gi, (lh, rh, rds) in enumerate(grp):
                            tr.op("T", lambda lh=lh, rh=rh, bank=bank, gi=gi, kc=kc:
                                  t.matmul(bank.ap32, lh, rh,
                                           start=(kc == 0 and gi == 0),
                                           stop=(kc == 8 and gi == 2)),
                                  reads=rds, writes=[bank],
                                  inc=(kc == 8 and gi == 2))
                    split(bank.ap32, Yh[:, h * 512:(h + 1) * 512],
                          Yl[:, h * 512:(h + 1) * 512], [bank], Yh, Yl,
                          eng_hi="S", eng_lo="V")

                def load_qh(ch, h):
                    ld(qhh[h % 2], qs_h[h, :, ch * WIN:(ch + 1) * WIN], qhh[h % 2])
                    ld(qhl[h % 2], qs_l[h, :, ch * WIN:(ch + 1) * WIN], qhl[h % 2])

                def load_kv(ch, kv):
                    ld(kvh[kv % 2][:, 0:1024],
                       ks_h[kv, :, ch * WIN: ch * WIN + 1024], kvh[kv % 2])
                    ld(kvh[kv % 2][:, 1024:KW], ks_h[kv, :, c.WKT:c.KVT],
                       kvh[kv % 2])
                    ld(kvl[kv % 2][:, 0:1024],
                       ks_l[kv, :, ch * WIN: ch * WIN + 1024], kvl[kv % 2])
                    ld(kvl[kv % 2][:, 1024:KW], ks_l[kv, :, c.WKT:c.KVT],
                       kvl[kv % 2])

                def load_vmask(ch):
                    for bi in range(9):
                        blk = c.NVB if bi == 8 else ch * 4 + bi
                        p = NMETA if bi == 8 else 128
                        ld(vch[0:p, bi * VW:(bi + 1) * VW], vs_h[blk, 0:p, :], vch)
                        ld(vcl[0:p, bi * VW:(bi + 1) * VW], vs_l[blk, 0:p, :], vcl)
                    ld(maskc.rearrange("p (a k) -> p a k", a=4),
                       mask_in[ch].rearrange("(a p) k -> p a k", p=128), maskc)

                load_vmask(0)
                load_qh(0, 0)
                load_kv(0, 0)
                for ch in range(c.NCH):
                    for h in range(c.NH):
                        if h + 1 < c.NH:
                            load_qh(ch, h + 1)
                        elif ch + 1 < c.NCH:
                            load_qh(ch + 1, 0)
                        if h % c.GQ == 0 and (h // c.GQ) + 1 < c.NKV:
                            load_kv(ch, h // c.GQ + 1)
                        elif (h % c.GQ == 0 and (h // c.GQ) + 1 == c.NKV
                              and ch + 1 < c.NCH and c.NKV % 2 == 0):
                            load_kv(ch + 1, 0)
                        emit_scores(ch, h, 0)
                        emit_scores(ch, h, 1)
                        if h > 0:
                            emit_av(ch, h - 1)
                        emit_softmax(ch, h, 0)
                        emit_scores(ch, h, 2)
                        emit_softmax(ch, h, 1)
                        emit_scores(ch, h, 3)
                        emit_softmax(ch, h, 2)
                        emit_softmax(ch, h, 3)
                    emit_av(ch, c.NH - 1)
                    if ch + 1 < c.NCH:
                        load_vmask(ch + 1)
                        if c.NKV % 2 == 1:
                            load_kv(ch + 1, 0)

                    nsl = c.E // 512
                    obanks = [banks[0], banks[1], banks[2], banks[3]]
                    for sl in range(nsl):
                        for d in range(c.DC):
                            wb = d % 4
                            ld(woh[wb], wos_h[d, :, sl * 512:(sl + 1) * 512], woh[wb])
                            ld(wol[wb], wos_l[d, :, sl * 512:(sl + 1) * 512], wol[wb])
                            for tsub in range(NQT):
                                sth = Yh[:, d * 512 + tsub * 128:
                                         d * 512 + (tsub + 1) * 128]
                                stl = Yl[:, d * 512 + tsub * 128:
                                         d * 512 + (tsub + 1) * 128]
                                bank = obanks[tsub]
                                grp = [(sth, woh[wb], [Yh, woh[wb]]),
                                       (sth, wol[wb], [Yh, wol[wb]]),
                                       (stl, woh[wb], [Yl, woh[wb]])]
                                for gi, (lh, rh, rds) in enumerate(grp):
                                    tr.op("T", lambda lh=lh, rh=rh, gi=gi, d=d,
                                          bank=bank: t.matmul(
                                              bank.ap32, lh, rh,
                                              start=(d == 0 and gi == 0),
                                              stop=(d == c.DC - 1 and gi == 2)),
                                          reads=rds, writes=[bank],
                                          inc=(d == c.DC - 1 and gi == 2))
                        for tsub in range(NQT):
                            bank = obanks[tsub]
                            tr.op("S", lambda sl=sl, tsub=tsub, bank=bank: s.activation(
                                x1pre[tsub][:, sl * 512:(sl + 1) * 512],
                                bank.ap32, AF.Copy),
                                reads=[bank], writes=[x1pre[tsub]])
                    for tsub in range(NQT):
                        xb_ = xbt[tsub % 2]
                        r0row = WIN + ch * WIN + tsub * 128
                        ld(xb_, xk[r0row:r0row + 128, :], xb_)
                        tr.op("V", lambda xb_=xb_: v.tensor_add(xb_, xb_, bob),
                              reads=[bob, xb_], writes=[xb_])
                        tr.op("V", lambda tsub=tsub, xb_=xb_: v.tensor_add(
                            x1pre[tsub], x1pre[tsub], xb_),
                            reads=[xb_, x1pre[tsub]], writes=[x1pre[tsub]])
                        emit_ln(tr, nc, x1pre[tsub], x1pre[tsub], lnsc, lndump, lnv,
                                c.E, [x1pre[tsub]], [x1pre[tsub]])
                        stc(x1s[ch * WIN + tsub * 128: ch * WIN + (tsub + 1) * 128, :],
                            x1pre[tsub], x1s, x1pre[tsub])
                tr.barrier()

            with ExitStack() as ph:
                psE = ph.enter_context(nc.psum_tensor("psE", [128, 1024], F16))[:]
                bankE = Bank(psE)
                wi32 = [sb(ph, f"wi32_{i}", c.E, I32) for i in range(3)]
                wc16 = [sb(ph, f"wc16_{i}", c.E, F16) for i in range(3)]
                wt16 = [sb(ph, f"wt16_{i}", c.E, F16) for i in range(3)]
                for ic in range(c.IC):
                    for wsel in range(2):
                        bsel = (ic * 2 + wsel) % 3
                        wi, wc, wt = wi32[bsel], wc16[bsel], wt16[bsel]
                        src_rows = gw if wsel == 0 else uw
                        ld(wi, src_rows[ic * 128:(ic + 1) * 128, :], wi)
                        ce = "V" if wsel == 0 else "G"
                        tr.op(ce, lambda wi=wi, wc=wc, ce=ce: tr.eng(ce).tensor_scalar(
                            wc, wi, -128, None, ALU.add), reads=[wi], writes=[wc])
                        for hcg in range(0, c.DC, 8):
                            nn = min(8, c.DC - hcg)
                            for q in range(nn):
                                tr.op("T", lambda wc=wc, hcg=hcg, q=q: t.transpose(
                                    psE[:, q * 128:(q + 1) * 128],
                                    wc[:, (hcg + q) * 128:(hcg + q + 1) * 128],
                                    identh),
                                    reads=[wc, identh], writes=[bankE],
                                    inc=(q == nn - 1))
                            tr.op("S", lambda wt=wt, hcg=hcg, nn=nn: s.copy(
                                wt[:, hcg * 128:(hcg + nn) * 128], psE[:, 0:nn * 128]),
                                reads=[bankE], writes=[wt])
                        stc((wgs if wsel == 0 else wus)[ic], wt,
                            wgs if wsel == 0 else wus, wt)
                DHALF = c.I // 2
                dwi = [sb(ph, f"dwi{i}", DHALF, I32) for i in range(3)]
                dwc = [sb(ph, f"dwc{i}", DHALF, F16) for i in range(3)]
                dwt = [sb(ph, f"dwt{i}", DHALF, F16) for i in range(3)]
                for hc in range(c.DC):
                    for half in range(2):
                        bsel = (hc * 2 + half) % 3
                        wi, wc, wt = dwi[bsel], dwc[bsel], dwt[bsel]
                        ld(wi, dw[hc * 128:(hc + 1) * 128,
                                  half * DHALF:(half + 1) * DHALF], wi)
                        hw_ = DHALF // 2
                        tr.op("V", lambda wi=wi, wc=wc, hw_=hw_: v.tensor_scalar(
                            wc[:, 0:hw_], wi[:, 0:hw_], -128, None, ALU.add),
                            reads=[wi], writes=[wc])
                        tr.op("G", lambda wi=wi, wc=wc, hw_=hw_: g.tensor_scalar(
                            wc[:, hw_:2 * hw_], wi[:, hw_:2 * hw_], -128, None,
                            ALU.add), reads=[wi], writes=[wc])
                        nic = DHALF // 128
                        for icg in range(0, nic, 8):
                            nn = min(8, nic - icg)
                            for q in range(nn):
                                tr.op("T", lambda wc=wc, icg=icg, q=q: t.transpose(
                                    psE[:, q * 128:(q + 1) * 128],
                                    wc[:, (icg + q) * 128:(icg + q + 1) * 128],
                                    identh),
                                    reads=[wc, identh], writes=[bankE],
                                    inc=(q == nn - 1))
                            tr.op("S", lambda wt=wt, icg=icg, nn=nn: s.copy(
                                wt[:, icg * 128:(icg + nn) * 128], psE[:, 0:nn * 128]),
                                reads=[bankE], writes=[wt])
                        ic0 = half * DHALF // 128
                        if nic >= c.IGRP:
                            g0 = ic0 // c.IGRP
                            ng = nic // c.IGRP
                            dst = wds[g0:g0 + ng, hc].rearrange("g p x -> p g x")
                            srcv = wt[:, 0:nic * 128].rearrange(
                                "p (g x) -> p g x", g=ng)
                            stc(dst, srcv, wds, wt)
                        else:
                            g0, i0 = ic0 // c.IGRP, ic0 % c.IGRP
                            stc(wds[g0, hc, :, i0 * 128:(i0 + nic) * 128],
                                wt[:, 0:nic * 128], wds, wt)
            tr.barrier()

            with ExitStack() as ph:
                psD = ph.enter_context(nc.psum_tensor("psD4", [128, 512], F32))[:]
                bankD = Bank(psD)
                x1tm = [sb(ph, f"x1tm{i}", c.E) for i in range(2)]
                x1Th = sb(ph, "x1Th", c.DC * 512, F16)
                x1Tl = sb(ph, "x1Tl", c.DC * 512, F16)
                x2acc = sb(ph, "x2acc", c.DC * 512)
                hacth = [sb(ph, f"hacth{i}", c.IGRP * 512, F16) for i in range(2)]
                hactl = [sb(ph, f"hactl{i}", c.IGRP * 512, F16) for i in range(2)]
                wgT = [sb(ph, f"wgT{i}", c.E, F16) for i in range(2)]
                wuT = [sb(ph, f"wuT{i}", c.E, F16) for i in range(2)]
                wdTt = [sb(ph, f"wdTt{i}", c.IGRP * 128, F16) for i in range(2)]
                silu_t = [sb(ph, f"silu{i}", 512) for i in range(2)]
                h32 = [sb(ph, f"h32_{i}", 512) for i in range(2)]
                outt = [sb(ph, f"outt{i}", c.E) for i in range(2)]
                lnsc = sb(ph, "lnsc4", c.E)
                lndump = sb(ph, "lndump4", c.E)
                lnv = [sb(ph, f"lnv4{i}", 1) for i in range(5)]

                TB = 512
                NTB = c.T // TB

                def emit_gu(tb, grp):
                    hh, hl = hacth[grp % 2], hactl[grp % 2]
                    for ii in range(c.IGRP):
                        ic = grp * c.IGRP + ii
                        wb = ic % 2
                        ld(wgT[wb], wgs[ic], wgT[wb])
                        ld(wuT[wb], wus[ic], wuT[wb])
                        for which, (bank, wT) in enumerate(
                                ((banks[6], wgT[wb]), (bankD, wuT[wb]))):
                            for hc in range(c.DC):
                                for gi, (x16, xbuf) in enumerate(
                                        ((x1Th, x1Th), (x1Tl, x1Tl))):
                                    tr.op("T", lambda wT=wT, x16=x16, hc=hc, gi=gi,
                                          bank=bank: t.matmul(
                                              bank.ap32,
                                              wT[:, hc * 128:(hc + 1) * 128],
                                              x16[:, hc * 512:(hc + 1) * 512],
                                              start=(hc == 0 and gi == 0),
                                              stop=(hc == c.DC - 1 and gi == 1)),
                                          reads=[wT, xbuf], writes=[bank],
                                          inc=(hc == c.DC - 1 and gi == 1))
                        sl_, hv = silu_t[ii % 2], h32[ii % 2]
                        tr.op("S", lambda sl_=sl_, ic=ic: s.activation(
                            sl_, banks[6].ap32, AF.Silu, scale=gsc[:, ic:ic + 1]),
                            reads=[banks[6], gsc], writes=[sl_])
                        tr.op("V", lambda hv=hv, sl_=sl_, ic=ic: v.scalar_tensor_tensor(
                            hv, psD, usc[:, ic:ic + 1], sl_, ALU.mult, ALU.mult),
                            reads=[bankD, usc, sl_], writes=[hv])
                        split(hv, hh[:, ii * 512:(ii + 1) * 512],
                              hl[:, ii * 512:(ii + 1) * 512], [hv], hh, hl, scale=HSC)

                def emit_down(tb, grp):
                    hh, hl = hacth[grp % 2], hactl[grp % 2]
                    for ho in range(c.DC):
                        wb = ho % 2
                        ld(wdTt[wb], wds[grp, ho], wdTt[wb])
                        bank = banks[ho % 4]
                        for ii in range(c.IGRP):
                            for gi, (h16, hbuf) in enumerate(((hh, hh), (hl, hl))):
                                tr.op("T", lambda h16=h16, ii=ii, wb=wb, gi=gi,
                                      bank=bank: t.matmul(
                                          bank.ap32,
                                          wdTt[wb][:, ii * 128:(ii + 1) * 128],
                                          h16[:, ii * 512:(ii + 1) * 512],
                                          start=(ii == 0 and gi == 0),
                                          stop=(ii == c.IGRP - 1 and gi == 1)),
                                      reads=[wdTt[wb], hbuf], writes=[bank],
                                      inc=(ii == c.IGRP - 1 and gi == 1))
                        if grp == 0:
                            tr.op("S", lambda ho=ho, bank=bank: s.activation(
                                x2acc[:, ho * 512:(ho + 1) * 512], bank.ap32,
                                AF.Copy, scale=dsc8[:, ho:ho + 1]),
                                reads=[bank, dsc8], writes=[x2acc])
                        else:
                            tr.op("V", lambda ho=ho, bank=bank: v.scalar_tensor_tensor(
                                x2acc[:, ho * 512:(ho + 1) * 512], bank.ap32,
                                dsc8[:, ho:ho + 1], x2acc[:, ho * 512:(ho + 1) * 512],
                                ALU.mult, ALU.add),
                                reads=[bank, dsc8, x2acc], writes=[x2acc])

                for tb in range(NTB):
                    for j in range(TB // 128):
                        xb_ = x1tm[j % 2]
                        ld(xb_, x1s[tb * TB + j * 128: tb * TB + (j + 1) * 128, :], xb_)
                        blocks = []
                        for dc in range(c.DC):
                            def dfn(buf16, dc=dc, j=j):
                                return buf16[:, dc * 512 + j * 128: dc * 512 + (j + 1) * 128]
                            blocks.append((dc * 128, dfn))
                        tp_split(xb_, xb_, blocks, x1Th, x1Tl, x1Th, x1Tl,
                                 bank_pick=lambda i: [banks[0], banks[1]][i])
                    for grp in range(c.NGRP):
                        emit_gu(tb, grp)
                        if grp > 0:
                            emit_down(tb, grp - 1)
                    emit_down(tb, c.NGRP - 1)

                    for tsub in range(TB // 128):
                        xr = x1tm[tsub % 2]
                        ld(xr, x1s[tb * TB + tsub * 128: tb * TB + (tsub + 1) * 128, :],
                           xr)
                        ot = outt[tsub % 2]
                        for hcg in range(0, c.DC, 4):
                            nn = min(4, c.DC - hcg)
                            bank = banks[(hcg // 4) % 2]
                            for q in range(nn):
                                tr.op("T", lambda hcg=hcg, q=q, tsub=tsub, bank=bank:
                                      t.transpose(bank.ap32[:, q * 128:(q + 1) * 128],
                                                  x2acc[:, (hcg + q) * 512 + tsub * 128:
                                                        (hcg + q) * 512 + (tsub + 1) * 128],
                                                  identf),
                                      reads=[x2acc, identf], writes=[bank],
                                      inc=(q == nn - 1))
                            tr.op("V", lambda hcg=hcg, nn=nn, ot=ot, xr=xr, bank=bank:
                                  v.tensor_add(ot[:, hcg * 128:(hcg + nn) * 128],
                                               bank.ap32[:, 0:nn * 128],
                                               xr[:, hcg * 128:(hcg + nn) * 128]),
                                  reads=[bank, xr], writes=[ot])
                        emit_ln(tr, nc, ot, ot, lnsc, lndump, lnv, c.E, [ot], [ot])
                        stc(y[tb * TB + tsub * 128: tb * TB + (tsub + 1) * 128, :],
                            ot, y, ot)
            tr.finish("Y")
    return nc


def emit_ln(tr, nc, src, dst, lnsc, lndump, lnv, E, src_bufs, dst_bufs):
    v, s = nc.vector, nc.scalar
    mean, ssq, varep, sd, rs = lnv
    tr.op("V", lambda: v.reduce_sum(mean, src, axis=AX.X), reads=src_bufs,
          writes=[mean])
    tr.op("V", lambda: v.tensor_scalar(mean, mean, 1.0 / E, None, ALU.mult),
          reads=[mean], writes=[mean])
    tr.op("V", lambda: v.tensor_scalar(lnsc, src, mean, None, ALU.subtract),
          reads=src_bufs + [mean], writes=[lnsc])
    tr.op("S", lambda: s.activation(lndump, lnsc, AF.Square, accum_out=ssq),
          reads=[lnsc], writes=[lndump, ssq])
    tr.op("V", lambda: v.tensor_scalar(varep, ssq, 1.0 / E, EPS, ALU.mult, ALU.add),
          reads=[ssq], writes=[varep])
    tr.op("S", lambda: s.activation(sd, varep, AF.Sqrt), reads=[varep], writes=[sd])
    tr.op("V", lambda: v.reciprocal(rs, sd), reads=[sd], writes=[rs])
    tr.op("V", lambda: v.tensor_scalar(dst, lnsc, rs, None, ALU.mult),
          reads=[lnsc, rs], writes=dst_bufs)



def make_masks(cfg, r0):
    c = cfg
    out = np.full((c.NCH, WIN, KW), NEG, dtype=np.float64)
    for chl in range(c.NCH):
        qsg = r0 + chl * WIN
        ls = max(NMETA, qsg - WIN + 1)
        q_pos = qsg + np.arange(WIN)[:, None]
        k_win = qsg - WIN + np.arange(2 * WIN)[None, :]
        member = k_win >= ls
        causal = k_win <= q_pos
        in_win = k_win >= q_pos - WIN + 1
        meta_f = k_win < NMETA
        allowed_w = member & causal & (in_win | meta_f) & (k_win >= 0)
        k_meta = np.arange(NMETA)[None, :]
        allowed_m = k_meta <= q_pos
        wpart = out[chl, :, 0:2 * WIN]
        wpart[allowed_w] = 0.0
        mpart = out[chl, :, 2 * WIN:]
        mpart[np.broadcast_to(allowed_m, mpart.shape)] = 0.0
    return out.astype(np.float16)


def _inv_freq_f32():
    y32 = (np.arange(0, HD, 2).astype(np.float32) / np.float32(HD))
    try:
        import ctypes
        libm = ctypes.CDLL("libm.so.6")
        libm.powf.restype = ctypes.c_float
        libm.powf.argtypes = [ctypes.c_float, ctypes.c_float]
        p = np.array([libm.powf(ctypes.c_float(10000.0), ctypes.c_float(v))
                      for v in y32], dtype=np.float32)
    except Exception:
        p = (np.float32(10000.0) ** y32).astype(np.float32)
    return (np.float32(1.0) / p).astype(np.float32)


_ROPE_CACHE = {}


def _rope_all(cfg):
    key = (cfg.S,)
    if key in _ROPE_CACHE:
        return _ROPE_CACHE[key]
    inv = _inv_freq_f32()
    pos = np.arange(-WIN, cfg.S).astype(np.float32)
    ang = (pos[:, None] * inv[None, :]).astype(np.float32)
    try:
        import ctypes
        libm = ctypes.CDLL("libm.so.6")
        libm.cosf.restype = ctypes.c_float
        libm.cosf.argtypes = [ctypes.c_float]
        libm.sinf.restype = ctypes.c_float
        libm.sinf.argtypes = [ctypes.c_float]
        flat = ang.ravel()
        cf, sf = libm.cosf, libm.sinf
        cfl = ctypes.c_float
        cos_t = np.fromiter((cf(cfl(a)) for a in flat), np.float32,
                            count=flat.size).reshape(ang.shape)
        sin_t = np.fromiter((sf(cfl(a)) for a in flat), np.float32,
                            count=flat.size).reshape(ang.shape)
    except Exception:
        cos_t = np.cos(ang.astype(np.float64)).astype(np.float32)
        sin_t = np.sin(ang.astype(np.float64)).astype(np.float32)
    _ROPE_CACHE[key] = (cos_t, sin_t)
    return cos_t, sin_t


def make_rope(cfg, r0):
    c = cfg
    cos_all, sin_all = _rope_all(c)
    pos_idx = np.concatenate([r0 - WIN + np.arange(c.WKT),
                              np.arange(NMETA)]) + WIN
    j = np.arange(HD)
    cos_t = cos_all[pos_idx][:, j // 2].T.astype(np.float64)
    sin_t = sin_all[pos_idx][:, j // 2].T.astype(np.float64)
    sign = np.where(j % 2 == 0, -1.0, 1.0)
    sin_t = sin_t * sign[:, None]
    return (np.ascontiguousarray(cos_t.astype(np.float32)),
            np.ascontiguousarray(sin_t.astype(np.float32)))


def make_core_inputs(cfg, core, x, Wq, bq, Wkv, bkv, Wo, bo,
                     gwv, gscv, uwv, uscv, dwv, dscv):
    c = cfg
    per_b = c.n_cores // c.B
    b = core // per_b
    r0 = (core % per_b) * c.T
    lo = r0 - WIN
    xw = np.zeros((c.WKT, c.E), dtype=np.float32)
    src0 = max(0, lo)
    xw[src0 - lo:, :] = x[b, src0: r0 + c.T, :]
    xkc = np.concatenate([xw, x[b, 0:NMETA, :]], axis=0)
    cos_t, sin_t = make_rope(c, r0)
    return {
        "xk": np.ascontiguousarray(xkc),
        "wq": Wq, "wkv": Wkv, "wo": Wo,
        "gw": gwv, "uw": uwv, "dw": dwv,
        "bq_t": np.ascontiguousarray(bq.reshape(c.DC, 128).T),
        "bkv_k": np.ascontiguousarray(bkv[0:c.NKV * HD].reshape(c.NKV, 128).T),
        "bkv_vb": np.ascontiguousarray(
            np.broadcast_to(bkv[c.NKV * HD:], (128, c.NKV * HD))),
        "bo_b": np.ascontiguousarray(np.broadcast_to(bo, (128, c.E))),
        "gsc_t": np.ascontiguousarray(gscv.reshape(c.IC, 128).T),
        "usc_t": np.ascontiguousarray(uscv.reshape(c.IC, 128).T),
        "dsc_t": np.ascontiguousarray(dscv.reshape(c.DC, 128).T),
        "cos_t": cos_t, "sin_t": sin_t,
        "mask_t": make_masks(c, r0),
        "identf": np.eye(128, dtype=np.float32),
        "identh": np.eye(128, dtype=np.float16),
    }


def kernel(L0_attn_x, L0_attn_Wq, L0_attn_bq, L0_attn_Wkv, L0_attn_bkv,
           L0_attn_Wo, L0_attn_bo, L0_ffn_gate_w, L0_ffn_gate_scale,
           L0_ffn_up_w, L0_ffn_up_scale, L0_ffn_down_w, L0_ffn_down_scale):
    cfg = FULL
    nc = build(cfg)
    x = np.asarray(L0_attn_x, dtype=np.float32)
    in_maps = [make_core_inputs(
        cfg, core, x,
        np.asarray(L0_attn_Wq, dtype=np.float32),
        np.asarray(L0_attn_bq, dtype=np.float32),
        np.asarray(L0_attn_Wkv, dtype=np.float32),
        np.asarray(L0_attn_bkv, dtype=np.float32),
        np.asarray(L0_attn_Wo, dtype=np.float32),
        np.asarray(L0_attn_bo, dtype=np.float32),
        np.asarray(L0_ffn_gate_w, dtype=np.int32),
        np.asarray(L0_ffn_gate_scale, dtype=np.float32),
        np.asarray(L0_ffn_up_w, dtype=np.int32),
        np.asarray(L0_ffn_up_scale, dtype=np.float32),
        np.asarray(L0_ffn_down_w, dtype=np.int32),
        np.asarray(L0_ffn_down_scale, dtype=np.float32))
        for core in range(cfg.n_cores)]
    res = run_bass_kernel_spmd(nc, in_maps, core_ids=list(range(cfg.n_cores)))
    out = np.empty((cfg.B, cfg.S, cfg.E), dtype=np.float32)
    per_b = cfg.n_cores // cfg.B
    for core in range(cfg.n_cores):
        b = core // per_b
        r0 = (core % per_b) * cfg.T
        out[b, r0:r0 + cfg.T, :] = res.results[core]["y"]
    return out


# version 18
# speedup vs baseline: 1.2624x; 1.2624x over previous
import math
from contextlib import ExitStack

import numpy as np

import concourse.bass as bass
import concourse.mybir as mybir
from concourse.bass_utils import run_bass_kernel_spmd

F32 = mybir.dt.float32
F16 = mybir.dt.float16
I32 = mybir.dt.int32
AF = mybir.ActivationFunctionType
ALU = mybir.AluOpType
AX = mybir.AxisListType

HD = 128
WIN = 512
NMETA = 16
KW = 2 * WIN + NMETA
NEG = -30000.0
ISCALE = 1.0 / math.sqrt(HD)
HSC = 0.125
EPS = 1e-5
PAIRSWAP = [i ^ 1 for i in range(32)]


class Cfg:
    def __init__(self, E=2048, NH=16, NKV=4, T=2048, I=8192, n_cores=8, S=8192, B=2):
        self.E, self.NH, self.NKV, self.T, self.I = E, NH, NKV, T, I
        self.n_cores, self.S, self.B = n_cores, S, B
        self.DC = E // 128
        self.NCH = T // WIN
        self.WKT = T + WIN
        self.KVT = self.WKT + NMETA
        self.IC = I // 128
        self.IGRP = min(8, self.IC)
        self.NGRP = self.IC // self.IGRP
        self.KVO = 2 * NKV * HD
        self.NVB = self.WKT // 128
        self.GQ = NH // NKV
        self.NKTW = self.WKT // WIN


FULL = Cfg()


class Tr:

    def __init__(self, nc, ctx):
        self.nc = nc
        self.ctx = ctx
        self.sems, self.counts, self.guar = {}, {}, {}
        self.writer, self.readers = {}, {}
        self.dma_sem_of_buf = {}
        self.n_dma_sems = 0
        self.pending_reads = {e: [] for e in ("T", "V", "S", "G", "Y")}
        for name in ("T", "V", "S", "G", "Y"):
            self.sems[name] = ctx.enter_context(nc.semaphore(f"sem{name}"))
            self.counts[name] = 0

    def eng(self, name):
        return {"T": self.nc.tensor, "V": self.nc.vector, "S": self.nc.scalar,
                "G": self.nc.gpsimd, "Y": self.nc.sync}[name]

    def _dma_sem(self, buf):
        name = self.dma_sem_of_buf.get(id(buf))
        if name is None:
            name = f"D{self.n_dma_sems}"
            self.n_dma_sems += 1
            self.sems[name] = self.ctx.enter_context(self.nc.semaphore(f"sem{name}"))
            self.counts[name] = 0
            self.dma_sem_of_buf[id(buf)] = name
        return name

    def _wait(self, eng_name, sem_name, val):
        if val <= 0:
            return
        if eng_name == "T" and sem_name == "T":
            return
        g = self.guar.setdefault(eng_name, {})
        if g.get(sem_name, 0) >= val:
            return
        self.eng(eng_name).wait_ge(self.sems[sem_name], val)
        g[sem_name] = val

    def op(self, eng_name, fn, reads=(), writes=(), dma=False, inc=True):
        for b in list(reads) + list(writes):
            w = self.writer.get(id(b))
            if w is not None:
                self._wait(eng_name, w[0], w[1])
        for b in writes:
            for (sm, vl) in self.readers.get(id(b), ()):
                self._wait(eng_name, sm, vl)
        instr = fn()
        if dma:
            assert len(writes) == 1
            sem_name = self._dma_sem(writes[0])
            self.counts[sem_name] += 16
            instr.then_inc(self.sems[sem_name], 16)
        else:
            sem_name = eng_name
            self.counts[sem_name] += 1
            instr.then_inc(self.sems[sem_name], 1)
        tok = (sem_name, self.counts[sem_name])
        for b in reads:
            self.readers.setdefault(id(b), []).append(tok)
        for b in writes:
            self.writer[id(b)] = tok
            self.readers[id(b)] = []
        return instr

    def barrier(self):
        for e in ("T", "V", "S", "G", "Y"):
            for sname, cval in list(self.counts.items()):
                if cval > 0:
                    self._wait(e, sname, cval)

    def finish(self, eng_name="Y"):
        for sname, cval in list(self.counts.items()):
            if cval > 0:
                self._wait(eng_name, sname, cval)


class Bank:
    def __init__(self, ap32):
        self.ap32 = ap32


def build(cfg):
    nc = bass.Bass("TRN2", target_bir_lowering=False)
    c = cfg

    xk = nc.dram_tensor("xk", [c.KVT, c.E], F32, kind="ExternalInput").ap()
    wq = nc.dram_tensor("wq", [c.E, c.E], F32, kind="ExternalInput").ap()
    wkv = nc.dram_tensor("wkv", [c.E, c.KVO], F32, kind="ExternalInput").ap()
    wo = nc.dram_tensor("wo", [c.E, c.E], F32, kind="ExternalInput").ap()
    gw = nc.dram_tensor("gw", [c.I, c.E], I32, kind="ExternalInput").ap()
    uw = nc.dram_tensor("uw", [c.I, c.E], I32, kind="ExternalInput").ap()
    dw = nc.dram_tensor("dw", [c.E, c.I], I32, kind="ExternalInput").ap()
    bq_in = nc.dram_tensor("bq_t", [128, c.DC], F32, kind="ExternalInput").ap()
    bkvk_in = nc.dram_tensor("bkv_k", [128, c.NKV], F32, kind="ExternalInput").ap()
    bkvv_in = nc.dram_tensor("bkv_vb", [128, c.NKV * HD], F32, kind="ExternalInput").ap()
    bob_in = nc.dram_tensor("bo_b", [128, c.E], F32, kind="ExternalInput").ap()
    gsc_in = nc.dram_tensor("gsc_t", [128, c.IC], F32, kind="ExternalInput").ap()
    usc_in = nc.dram_tensor("usc_t", [128, c.IC], F32, kind="ExternalInput").ap()
    dsc_in = nc.dram_tensor("dsc_t", [128, c.DC], F32, kind="ExternalInput").ap()
    cos_in = nc.dram_tensor("cos_t", [128, c.KVT], F32, kind="ExternalInput").ap()
    sin_in = nc.dram_tensor("sin_t", [128, c.KVT], F32, kind="ExternalInput").ap()
    mask_in = nc.dram_tensor("mask_t", [c.NCH, WIN, KW], F16, kind="ExternalInput").ap()
    idf_in = nc.dram_tensor("identf", [128, 128], F32, kind="ExternalInput").ap()
    idh_in = nc.dram_tensor("identh", [128, 128], F16, kind="ExternalInput").ap()
    y = nc.dram_tensor("y", [c.T, c.E], F32, kind="ExternalOutput").ap()

    qs_h = nc.dram_tensor("qs_h", [c.NH, 128, c.T], F16).ap()
    qs_l = nc.dram_tensor("qs_l", [c.NH, 128, c.T], F16).ap()
    ks_h = nc.dram_tensor("ks_h", [c.NKV, 128, c.KVT], F16).ap()
    ks_l = nc.dram_tensor("ks_l", [c.NKV, 128, c.KVT], F16).ap()
    vs_h = nc.dram_tensor("vs_h", [c.NVB + 1, 128, c.NKV * HD], F16).ap()
    vs_l = nc.dram_tensor("vs_l", [c.NVB + 1, 128, c.NKV * HD], F16).ap()
    x1s = nc.dram_tensor("x1s", [c.T, c.E], F32).ap()
    wqs_h = nc.dram_tensor("wqs_h", [c.DC, 128, c.E], F16).ap()
    wqs_l = nc.dram_tensor("wqs_l", [c.DC, 128, c.E], F16).ap()
    wkvs_h = nc.dram_tensor("wkvs_h", [c.DC, 128, c.KVO], F16).ap()
    wkvs_l = nc.dram_tensor("wkvs_l", [c.DC, 128, c.KVO], F16).ap()
    wos_h = nc.dram_tensor("wos_h", [c.DC, 128, c.E], F16).ap()
    wos_l = nc.dram_tensor("wos_l", [c.DC, 128, c.E], F16).ap()
    wgs = nc.dram_tensor("wgs", [c.IC, 128, c.E], F16).ap()
    wus = nc.dram_tensor("wus", [c.IC, 128, c.E], F16).ap()
    wds = nc.dram_tensor("wds", [c.NGRP, c.DC, 128, c.IGRP * 128], F16).ap()

    with ExitStack() as top:
        def sb(stk, n, cols, dt=F32, parts=128):
            return stk.enter_context(nc.sbuf_tensor(n, [parts, cols], dt))[:]

        psA = top.enter_context(nc.psum_tensor("psA", [128, 1536], F32))[:]
        psB = top.enter_context(nc.psum_tensor("psB", [128, 1536], F32))[:]
        psC = top.enter_context(nc.psum_tensor("psC", [128, 512], F32))[:]
        banks = [Bank(psA[:, j * 512:(j + 1) * 512]) for j in range(3)] + \
                [Bank(psB[:, j * 512:(j + 1) * 512]) for j in range(3)] + \
                [Bank(psC)]

        identf = sb(top, "c_identf", 128)
        identh = sb(top, "c_identh", 128, F16)
        bqt = sb(top, "bqt", c.DC)
        bkvk = sb(top, "bkvk", c.NKV)
        bkvv = sb(top, "bkvv", c.NKV * HD)
        bob = sb(top, "bob", c.E)
        gsc = sb(top, "gsc", c.IC)
        usc = sb(top, "usc", c.IC)
        dsc8 = sb(top, "dsc8", c.DC)
        dsc_raw = sb(top, "dsc_raw", c.DC)
        negc = sb(top, "negc", 1)

        with nc.Block():
            tr = Tr(nc, top)
            v, s, g, t = nc.vector, nc.scalar, nc.gpsimd, nc.tensor

            def ld(dst, src, buf, reads=(), eng="Y", slow=False):
                kw = {"allow_slow_non_contiguous": True} if slow else {}
                tr.op(eng, lambda: tr.eng(eng).dma_start(out=dst, in_=src, **kw),
                      reads=reads, writes=[buf], dma=True)

            def stc(dst, src, dstbuf, srcbuf, eng="G"):
                tr.op(eng, lambda: tr.eng(eng).dma_start(out=dst, in_=src),
                      reads=[srcbuf], writes=[dstbuf], dma=True)

            for dst, src in ((identf, idf_in), (identh, idh_in), (bqt, bq_in),
                             (bkvk, bkvk_in), (bkvv, bkvv_in), (bob, bob_in),
                             (gsc, gsc_in), (usc, usc_in), (dsc_raw, dsc_in)):
                ld(dst, src, dst)
            tr.op("V", lambda: v.tensor_scalar(dsc8, dsc_raw, 1.0 / HSC, None, ALU.mult),
                  reads=[dsc_raw], writes=[dsc8])
            tr.op("V", lambda: v.memset(negc, -20.0), writes=[negc])

            def split(src, hi, lo, rbufs, hbuf, lbuf, scale=None,
                      eng_hi="G", eng_lo="G"):
                rb = list(rbufs)
                if eng_hi == "S":
                    tr.op("S", lambda: s.activation(hi, src, AF.Copy,
                                                    scale=(1.0 if scale is None else scale)),
                          reads=rb, writes=[hbuf])
                elif scale is None:
                    tr.op(eng_hi, lambda: tr.eng(eng_hi).tensor_copy(hi, src),
                          reads=rb, writes=[hbuf])
                else:
                    tr.op(eng_hi, lambda: tr.eng(eng_hi).tensor_scalar(
                        hi, src, scale, None, ALU.mult), reads=rb, writes=[hbuf])
                if scale is None:
                    tr.op(eng_lo, lambda: tr.eng(eng_lo).tensor_sub(lo, src, hi),
                          reads=rb + [hbuf], writes=[lbuf])
                else:
                    el = "V" if eng_lo == "G" else eng_lo
                    tr.op(el, lambda: tr.eng(el).scalar_tensor_tensor(
                        lo, src, scale, hi, ALU.mult, ALU.subtract),
                        reads=rb + [hbuf], writes=[lbuf])

            def tp_split(src, srcbuf, blocks, dsth, dstl, hbuf, lbuf, p=128,
                         bank_pick=None):
                for i0 in range(0, len(blocks), 4):
                    grp = blocks[i0:i0 + 4]
                    bank = bank_pick((i0 // 4) % 2)
                    for j, (c0, _) in enumerate(grp):
                        tr.op("T", lambda c0=c0, j=j, bank=bank: t.transpose(
                            bank.ap32[:, j * 128: j * 128 + p],
                            src[0:p, c0:c0 + 128], identf[0:p, 0:p]),
                            reads=[srcbuf, identf], writes=[bank],
                            inc=(j == len(grp) - 1))
                    for j, (_, dfn) in enumerate(grp):
                        dh = dfn(dsth)
                        dl = dfn(dstl)
                        tr.op("S", lambda dh=dh, j=j, bank=bank: s.activation(
                            dh, bank.ap32[:, j * 128: j * 128 + p], AF.Copy),
                            reads=[bank], writes=[hbuf])
                        tr.op("V", lambda dh=dh, dl=dl, j=j, bank=bank: v.tensor_sub(
                            dl, bank.ap32[:, j * 128: j * 128 + p], dh),
                            reads=[bank, hbuf], writes=[lbuf])

            with ExitStack() as ph:
                w32 = [sb(ph, f"pw32_{i}", c.E) for i in range(2)]
                wsh = [sb(ph, f"pwsh{i}", c.E, F16) for i in range(2)]
                wsl = [sb(ph, f"pwsl{i}", c.E, F16) for i in range(2)]
                for d in range(c.DC):
                    for wi_, (src, dsth, dstl, cols) in enumerate(
                            ((wq, wqs_h, wqs_l, c.E),
                             (wkv, wkvs_h, wkvs_l, c.KVO),
                             (wo, wos_h, wos_l, c.E))):
                        bsel = (d * 3 + wi_) % 2
                        wb_, hb_, lb_ = w32[bsel], wsh[bsel], wsl[bsel]
                        ld(wb_[:, 0:cols], src[d * 128:(d + 1) * 128, :], wb_)
                        split(wb_[:, 0:cols], hb_[:, 0:cols], lb_[:, 0:cols],
                              [wb_], hb_, lb_, eng_hi="S", eng_lo="V")
                        stc(dsth[d], hb_[:, 0:cols], dsth, hb_)
                        stc(dstl[d], lb_[:, 0:cols], dstl, lb_)
            tr.barrier()

            with ExitStack() as ph:
                psD = ph.enter_context(nc.psum_tensor("psD", [128, 512], F32))[:]
                bankD = Bank(psD)
                cost = sb(ph, "cost", c.KVT)
                ssint = sb(ph, "ssint", c.KVT)
                ld(cost, cos_in, cost)
                ld(ssint, sin_in, ssint)
                xtm = [sb(ph, f"xtm{i}", c.E) for i in range(4)]
                xTh = sb(ph, "xTh", c.DC * 512, F16)
                xTl = sb(ph, "xTl", c.DC * 512, F16)
                wkvh = [sb(ph, f"wkvh{i}", c.KVO, F16) for i in range(4)]
                wkvl = [sb(ph, f"wkvl{i}", c.KVO, F16) for i in range(4)]
                wqh = [sb(ph, f"wqh{i}", 1024, F16) for i in range(6)]
                wql = [sb(ph, f"wql{i}", 1024, F16) for i in range(6)]
                ro = [sb(ph, f"ro{i}", 512) for i in range(9)]
                sp16 = [sb(ph, f"sp16_{i}", 512, F16) for i in range(8)]

                kbanks = [banks[0], banks[1], banks[2], banks[3]][:c.NKV]
                vbanks = [banks[4], banks[5], banks[6], bankD]

                ktiles = [(kt * WIN, WIN) for kt in range(c.NKTW)] + [(c.WKT, NMETA)]

                def emit_x_loads(ti):
                    slot0_, w_ = ktiles[ti]
                    for j in range((w_ + 127) // 128):
                        p_ = min(128, w_ - j * 128)
                        ld(xtm[j][0:p_, :],
                           xk[slot0_ + j * 128: slot0_ + j * 128 + p_, :], xtm[j])

                emit_x_loads(0)
                for ti, (slot0, w) in enumerate(ktiles):
                    nts = (w + 127) // 128
                    for j in range(nts):
                        p = min(128, w - j * 128)
                        xb_ = xtm[j]
                        blocks = []
                        for dc in range(c.DC):
                            def dfn(buf16, dc=dc, j=j, p=p):
                                return buf16[:, dc * 512 + j * 128: dc * 512 + j * 128 + p]
                            blocks.append((dc * 128, dfn))
                        tp_split(xb_, xb_, blocks, xTh, xTl, xTh, xTl, p=p,
                                 bank_pick=lambda i: [banks[0], banks[1]][i])
                    if ti + 1 < len(ktiles):
                        emit_x_loads(ti + 1)

                    nvs = nts
                    for d in range(c.DC):
                        wb = d % 4
                        ld(wkvh[wb], wkvs_h[d], wkvh[wb])
                        ld(wkvl[wb], wkvs_l[d], wkvl[wb])
                        xmv_h = xTh[:, d * 512: d * 512 + w]
                        xmv_l = xTl[:, d * 512: d * 512 + w]
                        for kv in range(c.NKV):
                            sth = wkvh[wb][:, kv * 128:(kv + 1) * 128]
                            stl = wkvl[wb][:, kv * 128:(kv + 1) * 128]
                            bank = kbanks[kv]
                            grp = [(sth, xmv_h, [wkvh[wb], xTh]),
                                   (sth, xmv_l, [wkvh[wb], xTl]),
                                   (stl, xmv_h, [wkvl[wb], xTh])]
                            for gi, (lh, rh, rds) in enumerate(grp):
                                tr.op("T", lambda lh=lh, rh=rh, bank=bank, w=w, gi=gi, d=d:
                                      t.matmul(bank.ap32[:, 0:w], lh, rh,
                                               start=(d == 0 and gi == 0),
                                               stop=(d == c.DC - 1 and gi == 2)),
                                      reads=rds, writes=[bank],
                                      inc=(d == c.DC - 1 and gi == 2))
                        mvvh = wkvh[wb][:, c.NKV * 128: 2 * c.NKV * 128]
                        mvvl = wkvl[wb][:, c.NKV * 128: 2 * c.NKV * 128]
                        for ks_ in range(nvs):
                            p = min(128, w - ks_ * 128)
                            sth = xTh[:, d * 512 + ks_ * 128: d * 512 + ks_ * 128 + p]
                            stl = xTl[:, d * 512 + ks_ * 128: d * 512 + ks_ * 128 + p]
                            bank = vbanks[ks_]
                            grp = [(sth, mvvh, [xTh, wkvh[wb]]),
                                   (sth, mvvl, [xTh, wkvl[wb]]),
                                   (stl, mvvh, [xTl, wkvh[wb]])]
                            for gi, (lh, rh, rds) in enumerate(grp):
                                tr.op("T", lambda lh=lh, rh=rh, bank=bank, p=p, gi=gi, d=d:
                                      t.matmul(bank.ap32[0:p, 0:c.NKV * HD], lh, rh,
                                               start=(d == 0 and gi == 0),
                                               stop=(d == c.DC - 1 and gi == 2)),
                                      reads=rds, writes=[bank],
                                      inc=(d == c.DC - 1 and gi == 2))

                    for kv in range(c.NKV):
                        bank = kbanks[kv]
                        kraw, ksf, t1, t2 = ro[0], ro[1], ro[2], ro[3]
                        tr.op("S", lambda bank=bank, kv=kv, w=w: s.activation(
                            kraw[:, 0:w], bank.ap32[:, 0:w], AF.Identity,
                            bias=bkvk[:, kv:kv + 1]),
                            reads=[bank, bkvk], writes=[kraw])
                        tr.op("V", lambda w=w: v.stream_shuffle(
                            ksf[:, 0:w], kraw[:, 0:w], PAIRSWAP),
                            reads=[kraw], writes=[ksf])
                        tr.op("V", lambda w=w, slot0=slot0: v.tensor_mul(
                            t1[:, 0:w], kraw[:, 0:w], cost[:, slot0:slot0 + w]),
                            reads=[kraw, cost], writes=[t1])
                        tr.op("V", lambda w=w, slot0=slot0: v.tensor_mul(
                            t2[:, 0:w], ksf[:, 0:w], ssint[:, slot0:slot0 + w]),
                            reads=[ksf, ssint], writes=[t2])
                        tr.op("V", lambda w=w: v.tensor_add(
                            t1[:, 0:w], t1[:, 0:w], t2[:, 0:w]),
                            reads=[t2, t1], writes=[t1])
                        kh16, kl16 = sp16[0], sp16[1]
                        split(t1[:, 0:w], kh16[:, 0:w], kl16[:, 0:w], [t1], kh16, kl16)
                        stc(ks_h[kv, :, slot0:slot0 + w], kh16[:, 0:w], ks_h, kh16)
                        stc(ks_l[kv, :, slot0:slot0 + w], kl16[:, 0:w], ks_l, kl16)

                    for ks_ in range(nvs):
                        p = min(128, w - ks_ * 128)
                        bank = vbanks[ks_]
                        vtmp = ro[4]
                        tr.op("V", lambda bank=bank, p=p: v.tensor_add(
                            vtmp[0:p, 0:c.NKV * HD], bank.ap32[0:p, 0:c.NKV * HD],
                            bkvv[0:p, :]),
                            reads=[bank, bkvv], writes=[vtmp])
                        vh16, vl16 = sp16[2], sp16[3]
                        split(vtmp[0:p, 0:c.NKV * HD], vh16[0:p, 0:c.NKV * HD],
                              vl16[0:p, 0:c.NKV * HD], [vtmp], vh16, vl16,
                              eng_hi="S", eng_lo="V")
                        blk = c.NVB if w == NMETA else (slot0 // 128 + ks_)
                        stc(vs_h[blk, 0:p, :], vh16[0:p, 0:c.NKV * HD], vs_h, vh16)
                        stc(vs_l[blk, 0:p, :], vl16[0:p, 0:c.NKV * HD], vs_l, vl16)

                    if slot0 >= WIN and w == WIN:
                        tok0 = slot0 - WIN
                        for half in range(0, c.DC, 8):
                            nh_ = min(8, c.DC - half)
                            qbanks = (banks + [bankD])[:nh_] if nh_ <= 7 else \
                                banks[:7] + [bankD]
                            for d in range(c.DC):
                                qb = d % 6
                                ld(wqh[qb][:, 0:nh_ * 128],
                                   wqs_h[d, :, half * 128:(half + nh_) * 128], wqh[qb])
                                ld(wql[qb][:, 0:nh_ * 128],
                                   wqs_l[d, :, half * 128:(half + nh_) * 128], wql[qb])
                                xmv_h = xTh[:, d * 512: d * 512 + w]
                                xmv_l = xTl[:, d * 512: d * 512 + w]
                                for dq in range(nh_):
                                    sth = wqh[qb][:, dq * 128:(dq + 1) * 128]
                                    stl = wql[qb][:, dq * 128:(dq + 1) * 128]
                                    bank = qbanks[dq]
                                    grp = [(sth, xmv_h, [wqh[qb], xTh]),
                                           (sth, xmv_l, [wqh[qb], xTl]),
                                           (stl, xmv_h, [wql[qb], xTh])]
                                    for gi, (lh, rh, rds) in enumerate(grp):
                                        tr.op("T", lambda lh=lh, rh=rh, bank=bank,
                                              gi=gi, d=d, w=w:
                                              t.matmul(bank.ap32[:, 0:w], lh, rh,
                                                       start=(d == 0 and gi == 0),
                                                       stop=(d == c.DC - 1 and gi == 2)),
                                              reads=rds, writes=[bank],
                                              inc=(d == c.DC - 1 and gi == 2))
                            for dq in range(nh_):
                                dout = half + dq
                                bank = qbanks[dq]
                                qraw, qsf, t1, t2 = ro[5], ro[6], ro[7], ro[8]
                                tr.op("S", lambda bank=bank, dout=dout, w=w: s.activation(
                                    qraw[:, 0:w], bank.ap32[:, 0:w], AF.Identity,
                                    bias=bqt[:, dout:dout + 1]),
                                    reads=[bank, bqt], writes=[qraw])
                                tr.op("V", lambda w=w: v.stream_shuffle(
                                    qsf[:, 0:w], qraw[:, 0:w], PAIRSWAP),
                                    reads=[qraw], writes=[qsf])
                                tr.op("V", lambda w=w, slot0=slot0: v.tensor_mul(
                                    t1[:, 0:w], qraw[:, 0:w], cost[:, slot0:slot0 + w]),
                                    reads=[qraw, cost], writes=[t1])
                                tr.op("V", lambda w=w, slot0=slot0: v.tensor_mul(
                                    t2[:, 0:w], qsf[:, 0:w], ssint[:, slot0:slot0 + w]),
                                    reads=[qsf, ssint], writes=[t2])
                                tr.op("V", lambda w=w: v.tensor_add(
                                    t1[:, 0:w], t1[:, 0:w], t2[:, 0:w]),
                                    reads=[t2, t1], writes=[t1])
                                qh16, ql16 = sp16[4], sp16[5]
                                split(t1[:, 0:w], qh16[:, 0:w], ql16[:, 0:w], [t1],
                                      qh16, ql16, scale=ISCALE, eng_hi="S", eng_lo="V")
                                stc(qs_h[dout, :, tok0:tok0 + w], qh16[:, 0:w],
                                    qs_h, qh16)
                                stc(qs_l[dout, :, tok0:tok0 + w], ql16[:, 0:w],
                                    qs_l, ql16)
                tr.barrier()

            with ExitStack() as ph:
                psD = ph.enter_context(nc.psum_tensor("psD2", [128, 512], F32))[:]
                bankD = Bank(psD)
                qhh = [sb(ph, f"qhh{i}", 512, F16) for i in range(2)]
                qhl = [sb(ph, f"qhl{i}", 512, F16) for i in range(2)]
                kvh = [sb(ph, f"kvh{i}", KW, F16) for i in range(2)]
                kvl = [sb(ph, f"kvl{i}", KW, F16) for i in range(2)]
                VW = c.NKV * HD
                vch = sb(ph, "vch", 9 * VW, F16)
                vcl = sb(ph, "vcl", 9 * VW, F16)
                maskc = sb(ph, "maskc", 4 * KW, F16)
                Sm = [sb(ph, f"Sm{i}", KW) for i in range(2)]
                Aex = [sb(ph, f"Aex{i}", KW) for i in range(2)]
                An = [sb(ph, f"An{i}", KW) for i in range(2)]
                nmax = [sb(ph, f"nmax{i}", 1) for i in range(2)]
                ssum = [sb(ph, f"ssum{i}", 1) for i in range(2)]
                rvec = [sb(ph, f"rvec{i}", 1) for i in range(2)]
                ATh = sb(ph, "ATh", 9 * 512, F16)
                ATl = sb(ph, "ATl", 9 * 512, F16)
                Yh = sb(ph, "Yh", c.NH * 512, F16)
                Yl = sb(ph, "Yl", c.NH * 512, F16)
                woh = [sb(ph, f"woh{i}", 512, F16) for i in range(4)]
                wol = [sb(ph, f"wol{i}", 512, F16) for i in range(4)]
                xbt = [sb(ph, f"xbt{i}", c.E) for i in range(2)]
                x1pre = [sb(ph, f"x1pre{i}", c.E) for i in range(4)]
                lnsc = sb(ph, "lnsc", c.E)
                lndump = sb(ph, "lndump", c.E, F16)
                lnv = [sb(ph, f"lnv{i}", 1) for i in range(5)]

                NQT = WIN // 128
                ATr = ATh.rearrange("p (kc q) -> p kc q", kc=9)
                ATlr = ATl.rearrange("p (kc q) -> p kc q", kc=9)

                def emit_scores(ch, h, qt):
                    kv = h // c.GQ
                    kb, lb = kvh[kv % 2], kvl[kv % 2]
                    qb, qlb = qhh[h % 2], qhl[h % 2]
                    sc3 = 0 if qt % 2 == 0 else 3
                    ps_sc = psA if sc3 == 0 else psB
                    sth = qb[:, qt * 128:(qt + 1) * 128]
                    stl = qlb[:, qt * 128:(qt + 1) * 128]
                    for si, (k0, kw_) in enumerate(((0, 512), (512, 512), (1024, NMETA))):
                        mvh = kb[:, k0:k0 + kw_]
                        mvl = lb[:, k0:k0 + kw_]
                        bank = banks[sc3 + si]
                        grp = [(sth, mvh, [qb, kb]), (sth, mvl, [qb, lb]),
                               (stl, mvh, [qlb, kb])]
                        for gi, (lh, rh, rds) in enumerate(grp):
                            tr.op("T", lambda lh=lh, rh=rh, k0=k0, kw_=kw_, gi=gi,
                                  ps_sc=ps_sc: t.matmul(
                                      ps_sc[:, k0:k0 + kw_], lh, rh,
                                      start=(gi == 0), stop=(gi == 2)),
                                  reads=rds, writes=[bank], inc=(gi == 2))

                def emit_softmax(ch, h, qt):
                    i2 = qt % 2
                    sc3 = 0 if qt % 2 == 0 else 3
                    ps_sc = psA if sc3 == 0 else psB
                    trip = [banks[sc3], banks[sc3 + 1], banks[sc3 + 2]]
                    tr.op("V", lambda: v.tensor_add(
                        Sm[i2][:, 0:KW], ps_sc[:, 0:KW],
                        maskc[:, qt * KW:(qt + 1) * KW]),
                        reads=trip + [maskc], writes=[Sm[i2]])
                    tr.op("S", lambda: s.activation(Aex[i2][:, 0:KW], Sm[i2][:, 0:KW],
                                                    AF.Exp, bias=negc,
                                                    accum_out=ssum[i2]),
                          reads=[Sm[i2], negc], writes=[Aex[i2], ssum[i2]])
                    tr.op("V", lambda: v.reciprocal(rvec[i2], ssum[i2]),
                          reads=[ssum[i2]], writes=[rvec[i2]])
                    Ann = An[i2]
                    tr.op("S", lambda Ann=Ann: s.activation(
                        Ann[:, 0:KW], Aex[i2][:, 0:KW], AF.Copy, scale=rvec[i2]),
                        reads=[Aex[i2], rvec[i2]], writes=[Ann])
                    for bi in range(2):
                        bank = trip[bi]
                        for j in range(4):
                            kc = bi * 4 + j
                            tr.op("T", lambda kc=kc, j=j, bank=bank: t.transpose(
                                bank.ap32[:, j * 128:(j + 1) * 128],
                                Ann[:, kc * 128:(kc + 1) * 128], identf),
                                reads=[Ann, identf], writes=[bank], inc=(j == 3))
                        dh = ATr[:, bi * 4:(bi + 1) * 4, qt * 128:(qt + 1) * 128]
                        dl = ATlr[:, bi * 4:(bi + 1) * 4, qt * 128:(qt + 1) * 128]
                        br = bank.ap32.rearrange("p (j q) -> p j q", j=4)
                        tr.op("S", lambda dh=dh, br=br, bank=bank: s.activation(
                            dh, br, AF.Copy), reads=[bank], writes=[ATh])
                        tr.op("V", lambda dh=dh, dl=dl, br=br, bank=bank: v.tensor_sub(
                            dl, br, dh), reads=[bank, ATh], writes=[ATl])
                    bank = trip[2]
                    tr.op("T", lambda bank=bank: t.transpose(
                        bank.ap32[0:NMETA, 0:128], Ann[:, 1024:1024 + NMETA],
                        identf), reads=[Ann, identf], writes=[bank])
                    dh = ATr[0:NMETA, 8:9, qt * 128:(qt + 1) * 128].squeeze(1)
                    dl = ATlr[0:NMETA, 8:9, qt * 128:(qt + 1) * 128].squeeze(1)
                    tr.op("S", lambda dh=dh, bank=bank: s.activation(
                        dh, bank.ap32[0:NMETA, 0:128], AF.Copy),
                        reads=[bank], writes=[ATh])
                    tr.op("V", lambda dh=dh, dl=dl, bank=bank: v.tensor_sub(
                        dl, bank.ap32[0:NMETA, 0:128], dh),
                        reads=[bank, ATh], writes=[ATl])

                def emit_av(ch, h):
                    kv = h // c.GQ
                    bank = banks[6] if h % 2 == 0 else bankD
                    for kc in range(9):
                        p = NMETA if kc == 8 else 128
                        sth = vch[0:p, kc * VW + kv * 128: kc * VW + (kv + 1) * 128]
                        stl = vcl[0:p, kc * VW + kv * 128: kc * VW + (kv + 1) * 128]
                        mvh = ATh[0:p, kc * 512:(kc + 1) * 512]
                        mvl = ATl[0:p, kc * 512:(kc + 1) * 512]
                        grp = [(sth, mvh, [vch, ATh]), (sth, mvl, [vch, ATl]),
                               (stl, mvh, [vcl, ATh])]
                        for gi, (lh, rh, rds) in enumerate(grp):
                            tr.op("T", lambda lh=lh, rh=rh, bank=bank, gi=gi, kc=kc:
                                  t.matmul(bank.ap32, lh, rh,
                                           start=(kc == 0 and gi == 0),
                                           stop=(kc == 8 and gi == 2)),
                                  reads=rds, writes=[bank],
                                  inc=(kc == 8 and gi == 2))
                    split(bank.ap32, Yh[:, h * 512:(h + 1) * 512],
                          Yl[:, h * 512:(h + 1) * 512], [bank], Yh, Yl,
                          eng_hi="S", eng_lo="V")

                def load_qh(ch, h):
                    ld(qhh[h % 2], qs_h[h, :, ch * WIN:(ch + 1) * WIN], qhh[h % 2])
                    ld(qhl[h % 2], qs_l[h, :, ch * WIN:(ch + 1) * WIN], qhl[h % 2])

                def load_kv(ch, kv):
                    ld(kvh[kv % 2][:, 0:1024],
                       ks_h[kv, :, ch * WIN: ch * WIN + 1024], kvh[kv % 2])
                    ld(kvh[kv % 2][:, 1024:KW], ks_h[kv, :, c.WKT:c.KVT],
                       kvh[kv % 2])
                    ld(kvl[kv % 2][:, 0:1024],
                       ks_l[kv, :, ch * WIN: ch * WIN + 1024], kvl[kv % 2])
                    ld(kvl[kv % 2][:, 1024:KW], ks_l[kv, :, c.WKT:c.KVT],
                       kvl[kv % 2])

                def load_vmask(ch):
                    for bi in range(9):
                        blk = c.NVB if bi == 8 else ch * 4 + bi
                        p = NMETA if bi == 8 else 128
                        ld(vch[0:p, bi * VW:(bi + 1) * VW], vs_h[blk, 0:p, :], vch)
                        ld(vcl[0:p, bi * VW:(bi + 1) * VW], vs_l[blk, 0:p, :], vcl)
                    ld(maskc.rearrange("p (a k) -> p a k", a=4),
                       mask_in[ch].rearrange("(a p) k -> p a k", p=128), maskc)

                load_vmask(0)
                load_qh(0, 0)
                load_kv(0, 0)
                for ch in range(c.NCH):
                    for h in range(c.NH):
                        if h + 1 < c.NH:
                            load_qh(ch, h + 1)
                        elif ch + 1 < c.NCH:
                            load_qh(ch + 1, 0)
                        if h % c.GQ == 0 and (h // c.GQ) + 1 < c.NKV:
                            load_kv(ch, h // c.GQ + 1)
                        elif (h % c.GQ == 0 and (h // c.GQ) + 1 == c.NKV
                              and ch + 1 < c.NCH and c.NKV % 2 == 0):
                            load_kv(ch + 1, 0)
                        emit_scores(ch, h, 0)
                        emit_scores(ch, h, 1)
                        if h > 0:
                            emit_av(ch, h - 1)
                        emit_softmax(ch, h, 0)
                        emit_scores(ch, h, 2)
                        emit_softmax(ch, h, 1)
                        emit_scores(ch, h, 3)
                        emit_softmax(ch, h, 2)
                        emit_softmax(ch, h, 3)
                    emit_av(ch, c.NH - 1)
                    if ch + 1 < c.NCH:
                        load_vmask(ch + 1)
                        if c.NKV % 2 == 1:
                            load_kv(ch + 1, 0)

                    nsl = c.E // 512
                    obanks = [banks[0], banks[1], banks[2], banks[3]]
                    for sl in range(nsl):
                        for d in range(c.DC):
                            wb = d % 4
                            ld(woh[wb], wos_h[d, :, sl * 512:(sl + 1) * 512], woh[wb])
                            ld(wol[wb], wos_l[d, :, sl * 512:(sl + 1) * 512], wol[wb])
                            for tsub in range(NQT):
                                sth = Yh[:, d * 512 + tsub * 128:
                                         d * 512 + (tsub + 1) * 128]
                                stl = Yl[:, d * 512 + tsub * 128:
                                         d * 512 + (tsub + 1) * 128]
                                bank = obanks[tsub]
                                grp = [(sth, woh[wb], [Yh, woh[wb]]),
                                       (sth, wol[wb], [Yh, wol[wb]]),
                                       (stl, woh[wb], [Yl, woh[wb]])]
                                for gi, (lh, rh, rds) in enumerate(grp):
                                    tr.op("T", lambda lh=lh, rh=rh, gi=gi, d=d,
                                          bank=bank: t.matmul(
                                              bank.ap32, lh, rh,
                                              start=(d == 0 and gi == 0),
                                              stop=(d == c.DC - 1 and gi == 2)),
                                          reads=rds, writes=[bank],
                                          inc=(d == c.DC - 1 and gi == 2))
                        for tsub in range(NQT):
                            bank = obanks[tsub]
                            tr.op("S", lambda sl=sl, tsub=tsub, bank=bank: s.activation(
                                x1pre[tsub][:, sl * 512:(sl + 1) * 512],
                                bank.ap32, AF.Copy),
                                reads=[bank], writes=[x1pre[tsub]])
                    for tsub in range(NQT):
                        xb_ = xbt[tsub % 2]
                        r0row = WIN + ch * WIN + tsub * 128
                        ld(xb_, xk[r0row:r0row + 128, :], xb_)
                        tr.op("V", lambda xb_=xb_: v.tensor_add(xb_, xb_, bob),
                              reads=[bob, xb_], writes=[xb_])
                        tr.op("V", lambda tsub=tsub, xb_=xb_: v.tensor_add(
                            x1pre[tsub], x1pre[tsub], xb_),
                            reads=[xb_, x1pre[tsub]], writes=[x1pre[tsub]])
                        emit_ln(tr, nc, x1pre[tsub], x1pre[tsub], lnsc, lndump, lnv,
                                c.E, [x1pre[tsub]], [x1pre[tsub]])
                        stc(x1s[ch * WIN + tsub * 128: ch * WIN + (tsub + 1) * 128, :],
                            x1pre[tsub], x1s, x1pre[tsub])
                tr.barrier()

            with ExitStack() as ph:
                psE = ph.enter_context(nc.psum_tensor("psE", [128, 1024], F16))[:]
                bankE = Bank(psE)
                wi32 = [sb(ph, f"wi32_{i}", c.E, I32) for i in range(2)]
                wc16 = [sb(ph, f"wc16_{i}", c.E, F16) for i in range(2)]
                wt16 = [sb(ph, f"wt16_{i}", c.E, F16) for i in range(2)]
                for ic in range(c.IC):
                    for wsel in range(2):
                        bsel = (ic * 2 + wsel) % 2
                        wi, wc, wt = wi32[bsel], wc16[bsel], wt16[bsel]
                        src_rows = gw if wsel == 0 else uw
                        ld(wi, src_rows[ic * 128:(ic + 1) * 128, :], wi)
                        tr.op("V", lambda wi=wi, wc=wc: v.tensor_scalar(
                            wc, wi, -128, None, ALU.add), reads=[wi], writes=[wc])
                        for hcg in range(0, c.DC, 8):
                            nn = min(8, c.DC - hcg)
                            for q in range(nn):
                                tr.op("T", lambda wc=wc, hcg=hcg, q=q: t.transpose(
                                    psE[:, q * 128:(q + 1) * 128],
                                    wc[:, (hcg + q) * 128:(hcg + q + 1) * 128],
                                    identh),
                                    reads=[wc, identh], writes=[bankE],
                                    inc=(q == nn - 1))
                            tr.op("S", lambda wt=wt, hcg=hcg, nn=nn: s.copy(
                                wt[:, hcg * 128:(hcg + nn) * 128], psE[:, 0:nn * 128]),
                                reads=[bankE], writes=[wt])
                        stc((wgs if wsel == 0 else wus)[ic], wt,
                            wgs if wsel == 0 else wus, wt)
                DHALF = c.I // 2
                dwi = [sb(ph, f"dwi{i}", DHALF, I32) for i in range(2)]
                dwc = [sb(ph, f"dwc{i}", DHALF, F16) for i in range(2)]
                dwt = [sb(ph, f"dwt{i}", DHALF, F16) for i in range(2)]
                for hc in range(c.DC):
                    for half in range(2):
                        bsel = (hc * 2 + half) % 2
                        wi, wc, wt = dwi[bsel], dwc[bsel], dwt[bsel]
                        ld(wi, dw[hc * 128:(hc + 1) * 128,
                                  half * DHALF:(half + 1) * DHALF], wi)
                        tr.op("V", lambda wi=wi, wc=wc: v.tensor_scalar(
                            wc, wi, -128, None, ALU.add), reads=[wi], writes=[wc])
                        nic = DHALF // 128
                        for icg in range(0, nic, 8):
                            nn = min(8, nic - icg)
                            for q in range(nn):
                                tr.op("T", lambda wc=wc, icg=icg, q=q: t.transpose(
                                    psE[:, q * 128:(q + 1) * 128],
                                    wc[:, (icg + q) * 128:(icg + q + 1) * 128],
                                    identh),
                                    reads=[wc, identh], writes=[bankE],
                                    inc=(q == nn - 1))
                            tr.op("S", lambda wt=wt, icg=icg, nn=nn: s.copy(
                                wt[:, icg * 128:(icg + nn) * 128], psE[:, 0:nn * 128]),
                                reads=[bankE], writes=[wt])
                        ic0 = half * DHALF // 128
                        if nic >= c.IGRP:
                            g0 = ic0 // c.IGRP
                            ng = nic // c.IGRP
                            dst = wds[g0:g0 + ng, hc].rearrange("g p x -> p g x")
                            srcv = wt[:, 0:nic * 128].rearrange(
                                "p (g x) -> p g x", g=ng)
                            stc(dst, srcv, wds, wt)
                        else:
                            g0, i0 = ic0 // c.IGRP, ic0 % c.IGRP
                            stc(wds[g0, hc, :, i0 * 128:(i0 + nic) * 128],
                                wt[:, 0:nic * 128], wds, wt)
            tr.barrier()

            with ExitStack() as ph:
                psD = ph.enter_context(nc.psum_tensor("psD4", [128, 512], F32))[:]
                bankD = Bank(psD)
                x1tm = [sb(ph, f"x1tm{i}", c.E) for i in range(2)]
                x1Th = sb(ph, "x1Th", c.DC * 512, F16)
                x1Tl = sb(ph, "x1Tl", c.DC * 512, F16)
                x2acc = sb(ph, "x2acc", c.DC * 512)
                hacth = [sb(ph, f"hacth{i}", c.IGRP * 512, F16) for i in range(2)]
                hactl = [sb(ph, f"hactl{i}", c.IGRP * 512, F16) for i in range(2)]
                wgT = [sb(ph, f"wgT{i}", c.E, F16) for i in range(2)]
                wuT = [sb(ph, f"wuT{i}", c.E, F16) for i in range(2)]
                wdTt = [sb(ph, f"wdTt{i}", c.IGRP * 128, F16) for i in range(2)]
                silu_t = [sb(ph, f"silu{i}", 512) for i in range(2)]
                h32 = [sb(ph, f"h32_{i}", 512) for i in range(2)]
                outt = [sb(ph, f"outt{i}", c.E) for i in range(2)]
                lnsc = sb(ph, "lnsc4", c.E)
                lndump = sb(ph, "lndump4", c.E)
                lnv = [sb(ph, f"lnv4{i}", 1) for i in range(5)]

                TB = 512
                NTB = c.T // TB

                def emit_gu(tb, grp):
                    hh, hl = hacth[grp % 2], hactl[grp % 2]
                    for ii in range(c.IGRP):
                        ic = grp * c.IGRP + ii
                        wb = ic % 2
                        ld(wgT[wb], wgs[ic], wgT[wb])
                        ld(wuT[wb], wus[ic], wuT[wb])
                        for which, (bank, wT) in enumerate(
                                ((banks[6], wgT[wb]), (bankD, wuT[wb]))):
                            for hc in range(c.DC):
                                for gi, (x16, xbuf) in enumerate(
                                        ((x1Th, x1Th), (x1Tl, x1Tl))):
                                    tr.op("T", lambda wT=wT, x16=x16, hc=hc, gi=gi,
                                          bank=bank: t.matmul(
                                              bank.ap32,
                                              wT[:, hc * 128:(hc + 1) * 128],
                                              x16[:, hc * 512:(hc + 1) * 512],
                                              start=(hc == 0 and gi == 0),
                                              stop=(hc == c.DC - 1 and gi == 1)),
                                          reads=[wT, xbuf], writes=[bank],
                                          inc=(hc == c.DC - 1 and gi == 1))
                        sl_, hv = silu_t[ii % 2], h32[ii % 2]
                        tr.op("S", lambda sl_=sl_, ic=ic: s.activation(
                            sl_, banks[6].ap32, AF.Silu, scale=gsc[:, ic:ic + 1]),
                            reads=[banks[6], gsc], writes=[sl_])
                        tr.op("V", lambda hv=hv, sl_=sl_, ic=ic: v.scalar_tensor_tensor(
                            hv, psD, usc[:, ic:ic + 1], sl_, ALU.mult, ALU.mult),
                            reads=[bankD, usc, sl_], writes=[hv])
                        split(hv, hh[:, ii * 512:(ii + 1) * 512],
                              hl[:, ii * 512:(ii + 1) * 512], [hv], hh, hl, scale=HSC)

                def emit_down(tb, grp):
                    hh, hl = hacth[grp % 2], hactl[grp % 2]
                    for ho in range(c.DC):
                        wb = ho % 2
                        ld(wdTt[wb], wds[grp, ho], wdTt[wb])
                        bank = banks[ho % 4]
                        for ii in range(c.IGRP):
                            for gi, (h16, hbuf) in enumerate(((hh, hh), (hl, hl))):
                                tr.op("T", lambda h16=h16, ii=ii, wb=wb, gi=gi,
                                      bank=bank: t.matmul(
                                          bank.ap32,
                                          wdTt[wb][:, ii * 128:(ii + 1) * 128],
                                          h16[:, ii * 512:(ii + 1) * 512],
                                          start=(ii == 0 and gi == 0),
                                          stop=(ii == c.IGRP - 1 and gi == 1)),
                                      reads=[wdTt[wb], hbuf], writes=[bank],
                                      inc=(ii == c.IGRP - 1 and gi == 1))
                        if grp == 0:
                            tr.op("S", lambda ho=ho, bank=bank: s.activation(
                                x2acc[:, ho * 512:(ho + 1) * 512], bank.ap32,
                                AF.Copy, scale=dsc8[:, ho:ho + 1]),
                                reads=[bank, dsc8], writes=[x2acc])
                        else:
                            tr.op("V", lambda ho=ho, bank=bank: v.scalar_tensor_tensor(
                                x2acc[:, ho * 512:(ho + 1) * 512], bank.ap32,
                                dsc8[:, ho:ho + 1], x2acc[:, ho * 512:(ho + 1) * 512],
                                ALU.mult, ALU.add),
                                reads=[bank, dsc8, x2acc], writes=[x2acc])

                for tb in range(NTB):
                    for j in range(TB // 128):
                        xb_ = x1tm[j % 2]
                        ld(xb_, x1s[tb * TB + j * 128: tb * TB + (j + 1) * 128, :], xb_)
                        blocks = []
                        for dc in range(c.DC):
                            def dfn(buf16, dc=dc, j=j):
                                return buf16[:, dc * 512 + j * 128: dc * 512 + (j + 1) * 128]
                            blocks.append((dc * 128, dfn))
                        tp_split(xb_, xb_, blocks, x1Th, x1Tl, x1Th, x1Tl,
                                 bank_pick=lambda i: [banks[0], banks[1]][i])
                    for grp in range(c.NGRP):
                        emit_gu(tb, grp)
                        if grp > 0:
                            emit_down(tb, grp - 1)
                    emit_down(tb, c.NGRP - 1)

                    for tsub in range(TB // 128):
                        xr = x1tm[tsub % 2]
                        ld(xr, x1s[tb * TB + tsub * 128: tb * TB + (tsub + 1) * 128, :],
                           xr)
                        ot = outt[tsub % 2]
                        for hcg in range(0, c.DC, 4):
                            nn = min(4, c.DC - hcg)
                            bank = banks[(hcg // 4) % 2]
                            for q in range(nn):
                                tr.op("T", lambda hcg=hcg, q=q, tsub=tsub, bank=bank:
                                      t.transpose(bank.ap32[:, q * 128:(q + 1) * 128],
                                                  x2acc[:, (hcg + q) * 512 + tsub * 128:
                                                        (hcg + q) * 512 + (tsub + 1) * 128],
                                                  identf),
                                      reads=[x2acc, identf], writes=[bank],
                                      inc=(q == nn - 1))
                            tr.op("V", lambda hcg=hcg, nn=nn, ot=ot, xr=xr, bank=bank:
                                  v.tensor_add(ot[:, hcg * 128:(hcg + nn) * 128],
                                               bank.ap32[:, 0:nn * 128],
                                               xr[:, hcg * 128:(hcg + nn) * 128]),
                                  reads=[bank, xr], writes=[ot])
                        emit_ln(tr, nc, ot, ot, lnsc, lndump, lnv, c.E, [ot], [ot])
                        stc(y[tb * TB + tsub * 128: tb * TB + (tsub + 1) * 128, :],
                            ot, y, ot)
            tr.finish("Y")
    return nc


def emit_ln(tr, nc, src, dst, lnsc, lndump, lnv, E, src_bufs, dst_bufs):
    v, s = nc.vector, nc.scalar
    mean, ssq, varep, sd, rs = lnv
    tr.op("V", lambda: v.reduce_sum(mean, src, axis=AX.X), reads=src_bufs,
          writes=[mean])
    tr.op("V", lambda: v.tensor_scalar(mean, mean, 1.0 / E, None, ALU.mult),
          reads=[mean], writes=[mean])
    tr.op("V", lambda: v.tensor_scalar(lnsc, src, mean, None, ALU.subtract),
          reads=src_bufs + [mean], writes=[lnsc])
    tr.op("S", lambda: s.activation(lndump, lnsc, AF.Square, accum_out=ssq),
          reads=[lnsc], writes=[lndump, ssq])
    tr.op("V", lambda: v.tensor_scalar(varep, ssq, 1.0 / E, EPS, ALU.mult, ALU.add),
          reads=[ssq], writes=[varep])
    tr.op("S", lambda: s.activation(sd, varep, AF.Sqrt), reads=[varep], writes=[sd])
    tr.op("V", lambda: v.reciprocal(rs, sd), reads=[sd], writes=[rs])
    tr.op("V", lambda: v.tensor_scalar(dst, lnsc, rs, None, ALU.mult),
          reads=[lnsc, rs], writes=dst_bufs)



def make_masks(cfg, r0):
    c = cfg
    out = np.full((c.NCH, WIN, KW), NEG, dtype=np.float64)
    for chl in range(c.NCH):
        qsg = r0 + chl * WIN
        ls = max(NMETA, qsg - WIN + 1)
        q_pos = qsg + np.arange(WIN)[:, None]
        k_win = qsg - WIN + np.arange(2 * WIN)[None, :]
        member = k_win >= ls
        causal = k_win <= q_pos
        in_win = k_win >= q_pos - WIN + 1
        meta_f = k_win < NMETA
        allowed_w = member & causal & (in_win | meta_f) & (k_win >= 0)
        k_meta = np.arange(NMETA)[None, :]
        allowed_m = k_meta <= q_pos
        wpart = out[chl, :, 0:2 * WIN]
        wpart[allowed_w] = 0.0
        mpart = out[chl, :, 2 * WIN:]
        mpart[np.broadcast_to(allowed_m, mpart.shape)] = 0.0
    return out.astype(np.float16)


def _inv_freq_f32():
    y32 = (np.arange(0, HD, 2).astype(np.float32) / np.float32(HD))
    try:
        import ctypes
        libm = ctypes.CDLL("libm.so.6")
        libm.powf.restype = ctypes.c_float
        libm.powf.argtypes = [ctypes.c_float, ctypes.c_float]
        p = np.array([libm.powf(ctypes.c_float(10000.0), ctypes.c_float(v))
                      for v in y32], dtype=np.float32)
    except Exception:
        p = (np.float32(10000.0) ** y32).astype(np.float32)
    return (np.float32(1.0) / p).astype(np.float32)


_ROPE_CACHE = {}


def _rope_all(cfg):
    key = (cfg.S,)
    if key in _ROPE_CACHE:
        return _ROPE_CACHE[key]
    inv = _inv_freq_f32()
    pos = np.arange(-WIN, cfg.S).astype(np.float32)
    ang = (pos[:, None] * inv[None, :]).astype(np.float32)
    try:
        import ctypes
        libm = ctypes.CDLL("libm.so.6")
        libm.cosf.restype = ctypes.c_float
        libm.cosf.argtypes = [ctypes.c_float]
        libm.sinf.restype = ctypes.c_float
        libm.sinf.argtypes = [ctypes.c_float]
        flat = ang.ravel()
        cf, sf = libm.cosf, libm.sinf
        cfl = ctypes.c_float
        cos_t = np.fromiter((cf(cfl(a)) for a in flat), np.float32,
                            count=flat.size).reshape(ang.shape)
        sin_t = np.fromiter((sf(cfl(a)) for a in flat), np.float32,
                            count=flat.size).reshape(ang.shape)
    except Exception:
        cos_t = np.cos(ang.astype(np.float64)).astype(np.float32)
        sin_t = np.sin(ang.astype(np.float64)).astype(np.float32)
    _ROPE_CACHE[key] = (cos_t, sin_t)
    return cos_t, sin_t


def make_rope(cfg, r0):
    c = cfg
    cos_all, sin_all = _rope_all(c)
    pos_idx = np.concatenate([r0 - WIN + np.arange(c.WKT),
                              np.arange(NMETA)]) + WIN
    j = np.arange(HD)
    cos_t = cos_all[pos_idx][:, j // 2].T.astype(np.float64)
    sin_t = sin_all[pos_idx][:, j // 2].T.astype(np.float64)
    sign = np.where(j % 2 == 0, -1.0, 1.0)
    sin_t = sin_t * sign[:, None]
    return (np.ascontiguousarray(cos_t.astype(np.float32)),
            np.ascontiguousarray(sin_t.astype(np.float32)))


def make_core_inputs(cfg, core, x, Wq, bq, Wkv, bkv, Wo, bo,
                     gwv, gscv, uwv, uscv, dwv, dscv):
    c = cfg
    per_b = c.n_cores // c.B
    b = core // per_b
    r0 = (core % per_b) * c.T
    lo = r0 - WIN
    xw = np.zeros((c.WKT, c.E), dtype=np.float32)
    src0 = max(0, lo)
    xw[src0 - lo:, :] = x[b, src0: r0 + c.T, :]
    xkc = np.concatenate([xw, x[b, 0:NMETA, :]], axis=0)
    cos_t, sin_t = make_rope(c, r0)
    return {
        "xk": np.ascontiguousarray(xkc),
        "wq": Wq, "wkv": Wkv, "wo": Wo,
        "gw": gwv, "uw": uwv, "dw": dwv,
        "bq_t": np.ascontiguousarray(bq.reshape(c.DC, 128).T),
        "bkv_k": np.ascontiguousarray(bkv[0:c.NKV * HD].reshape(c.NKV, 128).T),
        "bkv_vb": np.ascontiguousarray(
            np.broadcast_to(bkv[c.NKV * HD:], (128, c.NKV * HD))),
        "bo_b": np.ascontiguousarray(np.broadcast_to(bo, (128, c.E))),
        "gsc_t": np.ascontiguousarray(gscv.reshape(c.IC, 128).T),
        "usc_t": np.ascontiguousarray(uscv.reshape(c.IC, 128).T),
        "dsc_t": np.ascontiguousarray(dscv.reshape(c.DC, 128).T),
        "cos_t": cos_t, "sin_t": sin_t,
        "mask_t": make_masks(c, r0),
        "identf": np.eye(128, dtype=np.float32),
        "identh": np.eye(128, dtype=np.float16),
    }


def kernel(L0_attn_x, L0_attn_Wq, L0_attn_bq, L0_attn_Wkv, L0_attn_bkv,
           L0_attn_Wo, L0_attn_bo, L0_ffn_gate_w, L0_ffn_gate_scale,
           L0_ffn_up_w, L0_ffn_up_scale, L0_ffn_down_w, L0_ffn_down_scale):
    cfg = FULL
    nc = build(cfg)
    x = np.asarray(L0_attn_x, dtype=np.float32)
    in_maps = [make_core_inputs(
        cfg, core, x,
        np.asarray(L0_attn_Wq, dtype=np.float32),
        np.asarray(L0_attn_bq, dtype=np.float32),
        np.asarray(L0_attn_Wkv, dtype=np.float32),
        np.asarray(L0_attn_bkv, dtype=np.float32),
        np.asarray(L0_attn_Wo, dtype=np.float32),
        np.asarray(L0_attn_bo, dtype=np.float32),
        np.asarray(L0_ffn_gate_w, dtype=np.int32),
        np.asarray(L0_ffn_gate_scale, dtype=np.float32),
        np.asarray(L0_ffn_up_w, dtype=np.int32),
        np.asarray(L0_ffn_up_scale, dtype=np.float32),
        np.asarray(L0_ffn_down_w, dtype=np.int32),
        np.asarray(L0_ffn_down_scale, dtype=np.float32))
        for core in range(cfg.n_cores)]
    res = run_bass_kernel_spmd(nc, in_maps, core_ids=list(range(cfg.n_cores)))
    out = np.empty((cfg.B, cfg.S, cfg.E), dtype=np.float32)
    per_b = cfg.n_cores // cfg.B
    for core in range(cfg.n_cores):
        b = core // per_b
        r0 = (core % per_b) * cfg.T
        out[b, r0:r0 + cfg.T, :] = res.results[core]["y"]
    return out


# version 21
# speedup vs baseline: 1.3320x; 1.0551x over previous
import math
from contextlib import ExitStack

import numpy as np

import concourse.bass as bass
import concourse.mybir as mybir
from concourse.bass_utils import run_bass_kernel_spmd

F32 = mybir.dt.float32
F16 = mybir.dt.float16
I32 = mybir.dt.int32
AF = mybir.ActivationFunctionType
ALU = mybir.AluOpType
AX = mybir.AxisListType

HD = 128
WIN = 512
NMETA = 16
KW = 2 * WIN + NMETA
NEG = -30000.0
ISCALE = 1.0 / math.sqrt(HD)
HSC = 0.125
EPS = 1e-5
PAIRSWAP = [i ^ 1 for i in range(32)]


class Cfg:
    def __init__(self, E=2048, NH=16, NKV=4, T=2048, I=8192, n_cores=8, S=8192, B=2):
        self.E, self.NH, self.NKV, self.T, self.I = E, NH, NKV, T, I
        self.n_cores, self.S, self.B = n_cores, S, B
        self.DC = E // 128
        self.NCH = T // WIN
        self.WKT = T + WIN
        self.KVT = self.WKT + NMETA
        self.IC = I // 128
        self.IGRP = min(8, self.IC)
        self.NGRP = self.IC // self.IGRP
        self.KVO = 2 * NKV * HD
        self.NVB = self.WKT // 128
        self.GQ = NH // NKV
        self.NKTW = self.WKT // WIN


FULL = Cfg()


class Tr:

    def __init__(self, nc, ctx):
        self.nc = nc
        self.ctx = ctx
        self.sems, self.counts, self.guar = {}, {}, {}
        self.writer, self.readers = {}, {}
        self.dma_sem_of_buf = {}
        self.n_dma_sems = 0
        self.pending_reads = {e: [] for e in ("T", "V", "S", "G", "Y")}
        for name in ("T", "V", "S", "G", "Y"):
            self.sems[name] = ctx.enter_context(nc.semaphore(f"sem{name}"))
            self.counts[name] = 0

    def eng(self, name):
        return {"T": self.nc.tensor, "V": self.nc.vector, "S": self.nc.scalar,
                "G": self.nc.gpsimd, "Y": self.nc.sync}[name]

    def _dma_sem(self, buf):
        name = self.dma_sem_of_buf.get(id(buf))
        if name is None:
            name = f"D{self.n_dma_sems}"
            self.n_dma_sems += 1
            self.sems[name] = self.ctx.enter_context(self.nc.semaphore(f"sem{name}"))
            self.counts[name] = 0
            self.dma_sem_of_buf[id(buf)] = name
        return name

    def _wait(self, eng_name, sem_name, val):
        if val <= 0:
            return
        if eng_name == "T" and sem_name == "T":
            return
        g = self.guar.setdefault(eng_name, {})
        if g.get(sem_name, 0) >= val:
            return
        self.eng(eng_name).wait_ge(self.sems[sem_name], val)
        g[sem_name] = val

    def op(self, eng_name, fn, reads=(), writes=(), dma=False, inc=True):
        for b in list(reads) + list(writes):
            w = self.writer.get(id(b))
            if w is not None:
                self._wait(eng_name, w[0], w[1])
        for b in writes:
            for (sm, vl) in self.readers.get(id(b), ()):
                self._wait(eng_name, sm, vl)
        instr = fn()
        if dma:
            assert len(writes) == 1
            sem_name = self._dma_sem(writes[0])
            self.counts[sem_name] += 16
            instr.then_inc(self.sems[sem_name], 16)
        else:
            sem_name = eng_name
            self.counts[sem_name] += 1
            instr.then_inc(self.sems[sem_name], 1)
        tok = (sem_name, self.counts[sem_name])
        for b in reads:
            self.readers.setdefault(id(b), []).append(tok)
        for b in writes:
            self.writer[id(b)] = tok
            self.readers[id(b)] = []
        return instr

    def barrier(self):
        for e in ("T", "V", "S", "G", "Y"):
            for sname, cval in list(self.counts.items()):
                if cval > 0:
                    self._wait(e, sname, cval)

    def finish(self, eng_name="Y"):
        for sname, cval in list(self.counts.items()):
            if cval > 0:
                self._wait(eng_name, sname, cval)


class Bank:
    def __init__(self, ap32):
        self.ap32 = ap32


def build(cfg):
    nc = bass.Bass("TRN2", target_bir_lowering=False)
    c = cfg

    xk = nc.dram_tensor("xk", [c.KVT, c.E], F32, kind="ExternalInput").ap()
    wq = nc.dram_tensor("wq", [c.E, c.E], F32, kind="ExternalInput").ap()
    wkv = nc.dram_tensor("wkv", [c.E, c.KVO], F32, kind="ExternalInput").ap()
    wo = nc.dram_tensor("wo", [c.E, c.E], F32, kind="ExternalInput").ap()
    gw = nc.dram_tensor("gw", [c.I, c.E], I32, kind="ExternalInput").ap()
    uw = nc.dram_tensor("uw", [c.I, c.E], I32, kind="ExternalInput").ap()
    dw = nc.dram_tensor("dw", [c.E, c.I], I32, kind="ExternalInput").ap()
    bq_in = nc.dram_tensor("bq_t", [128, c.DC], F32, kind="ExternalInput").ap()
    bkvk_in = nc.dram_tensor("bkv_k", [128, c.NKV], F32, kind="ExternalInput").ap()
    bkvv_in = nc.dram_tensor("bkv_vb", [128, c.NKV * HD], F32, kind="ExternalInput").ap()
    bob_in = nc.dram_tensor("bo_b", [128, c.E], F32, kind="ExternalInput").ap()
    gsc_in = nc.dram_tensor("gsc_t", [128, c.IC], F32, kind="ExternalInput").ap()
    usc_in = nc.dram_tensor("usc_t", [128, c.IC], F32, kind="ExternalInput").ap()
    dsc_in = nc.dram_tensor("dsc_t", [128, c.DC], F32, kind="ExternalInput").ap()
    cos_in = nc.dram_tensor("cos_t", [128, c.KVT], F32, kind="ExternalInput").ap()
    sin_in = nc.dram_tensor("sin_t", [128, c.KVT], F32, kind="ExternalInput").ap()
    mask_in = nc.dram_tensor("mask_t", [c.NCH, WIN, KW], F16, kind="ExternalInput").ap()
    idf_in = nc.dram_tensor("identf", [128, 128], F32, kind="ExternalInput").ap()
    idh_in = nc.dram_tensor("identh", [128, 128], F16, kind="ExternalInput").ap()
    y = nc.dram_tensor("y", [c.T, c.E], F32, kind="ExternalOutput").ap()

    qs_h = nc.dram_tensor("qs_h", [c.NH, 128, c.T], F16).ap()
    qs_l = nc.dram_tensor("qs_l", [c.NH, 128, c.T], F16).ap()
    ks_h = nc.dram_tensor("ks_h", [c.NKV, 128, c.KVT], F16).ap()
    ks_l = nc.dram_tensor("ks_l", [c.NKV, 128, c.KVT], F16).ap()
    vs_h = nc.dram_tensor("vs_h", [c.NVB + 1, 128, c.NKV * HD], F16).ap()
    vs_l = nc.dram_tensor("vs_l", [c.NVB + 1, 128, c.NKV * HD], F16).ap()
    x1s = nc.dram_tensor("x1s", [c.T, c.E], F32).ap()
    wqs_h = nc.dram_tensor("wqs_h", [c.DC, 128, c.E], F16).ap()
    wqs_l = nc.dram_tensor("wqs_l", [c.DC, 128, c.E], F16).ap()
    wkvs_h = nc.dram_tensor("wkvs_h", [c.DC, 128, c.KVO], F16).ap()
    wkvs_l = nc.dram_tensor("wkvs_l", [c.DC, 128, c.KVO], F16).ap()
    wos_h = nc.dram_tensor("wos_h", [c.DC, 128, c.E], F16).ap()
    wos_l = nc.dram_tensor("wos_l", [c.DC, 128, c.E], F16).ap()
    wgs = nc.dram_tensor("wgs", [c.IC, 128, c.E], F16).ap()
    wus = nc.dram_tensor("wus", [c.IC, 128, c.E], F16).ap()
    wds = nc.dram_tensor("wds", [c.NGRP, c.DC, 128, c.IGRP * 128], F16).ap()

    with ExitStack() as top:
        def sb(stk, n, cols, dt=F32, parts=128):
            return stk.enter_context(nc.sbuf_tensor(n, [parts, cols], dt))[:]

        psA = top.enter_context(nc.psum_tensor("psA", [128, 1536], F32))[:]
        psB = top.enter_context(nc.psum_tensor("psB", [128, 1536], F32))[:]
        psC = top.enter_context(nc.psum_tensor("psC", [128, 512], F32))[:]
        banks = [Bank(psA[:, j * 512:(j + 1) * 512]) for j in range(3)] + \
                [Bank(psB[:, j * 512:(j + 1) * 512]) for j in range(3)] + \
                [Bank(psC)]

        identf = sb(top, "c_identf", 128)
        identh = sb(top, "c_identh", 128, F16)
        bqt = sb(top, "bqt", c.DC)
        bkvk = sb(top, "bkvk", c.NKV)
        bkvv = sb(top, "bkvv", c.NKV * HD)
        bob = sb(top, "bob", c.E)
        gsc = sb(top, "gsc", c.IC)
        usc = sb(top, "usc", c.IC)
        dsc8 = sb(top, "dsc8", c.DC)
        dsc_raw = sb(top, "dsc_raw", c.DC)
        negc = sb(top, "negc", 1)

        with nc.Block():
            tr = Tr(nc, top)
            v, s, g, t = nc.vector, nc.scalar, nc.gpsimd, nc.tensor

            def ld(dst, src, buf, reads=(), eng="Y", slow=False):
                kw = {"allow_slow_non_contiguous": True} if slow else {}
                tr.op(eng, lambda: tr.eng(eng).dma_start(out=dst, in_=src, **kw),
                      reads=reads, writes=[buf], dma=True)

            def stc(dst, src, dstbuf, srcbuf, eng="G"):
                tr.op(eng, lambda: tr.eng(eng).dma_start(out=dst, in_=src),
                      reads=[srcbuf], writes=[dstbuf], dma=True)

            for dst, src in ((identf, idf_in), (identh, idh_in), (bqt, bq_in),
                             (bkvk, bkvk_in), (bkvv, bkvv_in), (bob, bob_in),
                             (gsc, gsc_in), (usc, usc_in), (dsc_raw, dsc_in)):
                ld(dst, src, dst)
            tr.op("V", lambda: v.tensor_scalar(dsc8, dsc_raw, 1.0 / HSC, None, ALU.mult),
                  reads=[dsc_raw], writes=[dsc8])
            tr.op("V", lambda: v.memset(negc, -20.0), writes=[negc])

            def split(src, hi, lo, rbufs, hbuf, lbuf, scale=None,
                      eng_hi="G", eng_lo="G"):
                rb = list(rbufs)
                if eng_hi == "S":
                    tr.op("S", lambda: s.activation(hi, src, AF.Copy,
                                                    scale=(1.0 if scale is None else scale)),
                          reads=rb, writes=[hbuf])
                elif scale is None:
                    tr.op(eng_hi, lambda: tr.eng(eng_hi).tensor_copy(hi, src),
                          reads=rb, writes=[hbuf])
                else:
                    tr.op(eng_hi, lambda: tr.eng(eng_hi).tensor_scalar(
                        hi, src, scale, None, ALU.mult), reads=rb, writes=[hbuf])
                if scale is None:
                    tr.op(eng_lo, lambda: tr.eng(eng_lo).tensor_sub(lo, src, hi),
                          reads=rb + [hbuf], writes=[lbuf])
                else:
                    el = "V" if eng_lo == "G" else eng_lo
                    tr.op(el, lambda: tr.eng(el).scalar_tensor_tensor(
                        lo, src, scale, hi, ALU.mult, ALU.subtract),
                        reads=rb + [hbuf], writes=[lbuf])

            def tp_split(src, srcbuf, blocks, dsth, dstl, hbuf, lbuf, p=128,
                         bank_pick=None):
                for i0 in range(0, len(blocks), 4):
                    grp = blocks[i0:i0 + 4]
                    bank = bank_pick((i0 // 4) % 2)
                    for j, (c0, _) in enumerate(grp):
                        tr.op("T", lambda c0=c0, j=j, bank=bank: t.transpose(
                            bank.ap32[:, j * 128: j * 128 + p],
                            src[0:p, c0:c0 + 128], identf[0:p, 0:p]),
                            reads=[srcbuf, identf], writes=[bank],
                            inc=(j == len(grp) - 1))
                    for j, (_, dfn) in enumerate(grp):
                        dh = dfn(dsth)
                        dl = dfn(dstl)
                        tr.op("S", lambda dh=dh, j=j, bank=bank: s.activation(
                            dh, bank.ap32[:, j * 128: j * 128 + p], AF.Copy),
                            reads=[bank], writes=[hbuf])
                        tr.op("V", lambda dh=dh, dl=dl, j=j, bank=bank: v.tensor_sub(
                            dl, bank.ap32[:, j * 128: j * 128 + p], dh),
                            reads=[bank, hbuf], writes=[lbuf])

            with ExitStack() as ph:
                w32 = [sb(ph, f"pw32_{i}", c.E) for i in range(2)]
                wsh = [sb(ph, f"pwsh{i}", c.E, F16) for i in range(2)]
                wsl = [sb(ph, f"pwsl{i}", c.E, F16) for i in range(2)]
                for d in range(c.DC):
                    for wi_, (src, dsth, dstl, cols) in enumerate(
                            ((wq, wqs_h, wqs_l, c.E),
                             (wkv, wkvs_h, wkvs_l, c.KVO),
                             (wo, wos_h, wos_l, c.E))):
                        bsel = (d * 3 + wi_) % 2
                        wb_, hb_, lb_ = w32[bsel], wsh[bsel], wsl[bsel]
                        ld(wb_[:, 0:cols], src[d * 128:(d + 1) * 128, :], wb_)
                        split(wb_[:, 0:cols], hb_[:, 0:cols], lb_[:, 0:cols],
                              [wb_], hb_, lb_, eng_hi="S", eng_lo="V")
                        stc(dsth[d], hb_[:, 0:cols], dsth, hb_)
                        stc(dstl[d], lb_[:, 0:cols], dstl, lb_)
            tr.barrier()

            with ExitStack() as ph:
                psD = ph.enter_context(nc.psum_tensor("psD", [128, 512], F32))[:]
                bankD = Bank(psD)
                cost = sb(ph, "cost", c.KVT)
                ssint = sb(ph, "ssint", c.KVT)
                ld(cost, cos_in, cost)
                ld(ssint, sin_in, ssint)
                xtm = [sb(ph, f"xtm{i}", c.E) for i in range(4)]
                xTh = sb(ph, "xTh", c.DC * 512, F16)
                xTl = sb(ph, "xTl", c.DC * 512, F16)
                wkvh = [sb(ph, f"wkvh{i}", c.KVO, F16) for i in range(4)]
                wkvl = [sb(ph, f"wkvl{i}", c.KVO, F16) for i in range(4)]
                wqh = [sb(ph, f"wqh{i}", 1024, F16) for i in range(6)]
                wql = [sb(ph, f"wql{i}", 1024, F16) for i in range(6)]
                ro = [sb(ph, f"ro{i}", 512) for i in range(9)]
                sp16 = [sb(ph, f"sp16_{i}", 512, F16) for i in range(8)]

                kbanks = [banks[0], banks[1], banks[2], banks[3]][:c.NKV]
                vbanks = [banks[4], banks[5], banks[6], bankD]

                ktiles = [(kt * WIN, WIN) for kt in range(c.NKTW)] + [(c.WKT, NMETA)]

                def emit_x_loads(ti):
                    slot0_, w_ = ktiles[ti]
                    for j in range((w_ + 127) // 128):
                        p_ = min(128, w_ - j * 128)
                        ld(xtm[j][0:p_, :],
                           xk[slot0_ + j * 128: slot0_ + j * 128 + p_, :], xtm[j])

                emit_x_loads(0)
                for ti, (slot0, w) in enumerate(ktiles):
                    nts = (w + 127) // 128
                    for j in range(nts):
                        p = min(128, w - j * 128)
                        xb_ = xtm[j]
                        blocks = []
                        for dc in range(c.DC):
                            def dfn(buf16, dc=dc, j=j, p=p):
                                return buf16[:, dc * 512 + j * 128: dc * 512 + j * 128 + p]
                            blocks.append((dc * 128, dfn))
                        tp_split(xb_, xb_, blocks, xTh, xTl, xTh, xTl, p=p,
                                 bank_pick=lambda i: [banks[0], banks[1]][i])
                    if ti + 1 < len(ktiles):
                        emit_x_loads(ti + 1)

                    nvs = nts
                    for d in range(c.DC):
                        wb = d % 4
                        ld(wkvh[wb], wkvs_h[d], wkvh[wb])
                        ld(wkvl[wb], wkvs_l[d], wkvl[wb])
                        xmv_h = xTh[:, d * 512: d * 512 + w]
                        xmv_l = xTl[:, d * 512: d * 512 + w]
                        for kv in range(c.NKV):
                            sth = wkvh[wb][:, kv * 128:(kv + 1) * 128]
                            stl = wkvl[wb][:, kv * 128:(kv + 1) * 128]
                            bank = kbanks[kv]
                            grp = [(sth, xmv_h, [wkvh[wb], xTh]),
                                   (sth, xmv_l, [wkvh[wb], xTl]),
                                   (stl, xmv_h, [wkvl[wb], xTh])]
                            for gi, (lh, rh, rds) in enumerate(grp):
                                tr.op("T", lambda lh=lh, rh=rh, bank=bank, w=w, gi=gi, d=d:
                                      t.matmul(bank.ap32[:, 0:w], lh, rh,
                                               start=(d == 0 and gi == 0),
                                               stop=(d == c.DC - 1 and gi == 2)),
                                      reads=rds, writes=[bank],
                                      inc=(d == c.DC - 1 and gi == 2))
                        mvvh = wkvh[wb][:, c.NKV * 128: 2 * c.NKV * 128]
                        mvvl = wkvl[wb][:, c.NKV * 128: 2 * c.NKV * 128]
                        for ks_ in range(nvs):
                            p = min(128, w - ks_ * 128)
                            sth = xTh[:, d * 512 + ks_ * 128: d * 512 + ks_ * 128 + p]
                            stl = xTl[:, d * 512 + ks_ * 128: d * 512 + ks_ * 128 + p]
                            bank = vbanks[ks_]
                            grp = [(sth, mvvh, [xTh, wkvh[wb]]),
                                   (sth, mvvl, [xTh, wkvl[wb]]),
                                   (stl, mvvh, [xTl, wkvh[wb]])]
                            for gi, (lh, rh, rds) in enumerate(grp):
                                tr.op("T", lambda lh=lh, rh=rh, bank=bank, p=p, gi=gi, d=d:
                                      t.matmul(bank.ap32[0:p, 0:c.NKV * HD], lh, rh,
                                               start=(d == 0 and gi == 0),
                                               stop=(d == c.DC - 1 and gi == 2)),
                                      reads=rds, writes=[bank],
                                      inc=(d == c.DC - 1 and gi == 2))

                    for kv in range(c.NKV):
                        bank = kbanks[kv]
                        kraw, ksf, t1, t2 = ro[0], ro[1], ro[2], ro[3]
                        tr.op("S", lambda bank=bank, kv=kv, w=w: s.activation(
                            kraw[:, 0:w], bank.ap32[:, 0:w], AF.Identity,
                            bias=bkvk[:, kv:kv + 1]),
                            reads=[bank, bkvk], writes=[kraw])
                        tr.op("V", lambda w=w: v.stream_shuffle(
                            ksf[:, 0:w], kraw[:, 0:w], PAIRSWAP),
                            reads=[kraw], writes=[ksf])
                        tr.op("V", lambda w=w, slot0=slot0: v.tensor_mul(
                            t1[:, 0:w], kraw[:, 0:w], cost[:, slot0:slot0 + w]),
                            reads=[kraw, cost], writes=[t1])
                        tr.op("V", lambda w=w, slot0=slot0: v.tensor_mul(
                            t2[:, 0:w], ksf[:, 0:w], ssint[:, slot0:slot0 + w]),
                            reads=[ksf, ssint], writes=[t2])
                        tr.op("V", lambda w=w: v.tensor_add(
                            t1[:, 0:w], t1[:, 0:w], t2[:, 0:w]),
                            reads=[t2, t1], writes=[t1])
                        kh16, kl16 = sp16[0], sp16[1]
                        split(t1[:, 0:w], kh16[:, 0:w], kl16[:, 0:w], [t1], kh16, kl16)
                        stc(ks_h[kv, :, slot0:slot0 + w], kh16[:, 0:w], ks_h, kh16)
                        stc(ks_l[kv, :, slot0:slot0 + w], kl16[:, 0:w], ks_l, kl16)

                    for ks_ in range(nvs):
                        p = min(128, w - ks_ * 128)
                        bank = vbanks[ks_]
                        vtmp = ro[4]
                        tr.op("V", lambda bank=bank, p=p: v.tensor_add(
                            vtmp[0:p, 0:c.NKV * HD], bank.ap32[0:p, 0:c.NKV * HD],
                            bkvv[0:p, :]),
                            reads=[bank, bkvv], writes=[vtmp])
                        vh16, vl16 = sp16[2], sp16[3]
                        split(vtmp[0:p, 0:c.NKV * HD], vh16[0:p, 0:c.NKV * HD],
                              vl16[0:p, 0:c.NKV * HD], [vtmp], vh16, vl16,
                              eng_hi="S", eng_lo="V")
                        blk = c.NVB if w == NMETA else (slot0 // 128 + ks_)
                        stc(vs_h[blk, 0:p, :], vh16[0:p, 0:c.NKV * HD], vs_h, vh16)
                        stc(vs_l[blk, 0:p, :], vl16[0:p, 0:c.NKV * HD], vs_l, vl16)

                    if slot0 >= WIN and w == WIN:
                        tok0 = slot0 - WIN
                        for half in range(0, c.DC, 8):
                            nh_ = min(8, c.DC - half)
                            qbanks = (banks + [bankD])[:nh_] if nh_ <= 7 else \
                                banks[:7] + [bankD]
                            for d in range(c.DC):
                                qb = d % 6
                                ld(wqh[qb][:, 0:nh_ * 128],
                                   wqs_h[d, :, half * 128:(half + nh_) * 128], wqh[qb])
                                ld(wql[qb][:, 0:nh_ * 128],
                                   wqs_l[d, :, half * 128:(half + nh_) * 128], wql[qb])
                                xmv_h = xTh[:, d * 512: d * 512 + w]
                                xmv_l = xTl[:, d * 512: d * 512 + w]
                                for dq in range(nh_):
                                    sth = wqh[qb][:, dq * 128:(dq + 1) * 128]
                                    stl = wql[qb][:, dq * 128:(dq + 1) * 128]
                                    bank = qbanks[dq]
                                    grp = [(sth, xmv_h, [wqh[qb], xTh]),
                                           (sth, xmv_l, [wqh[qb], xTl]),
                                           (stl, xmv_h, [wql[qb], xTh])]
                                    for gi, (lh, rh, rds) in enumerate(grp):
                                        tr.op("T", lambda lh=lh, rh=rh, bank=bank,
                                              gi=gi, d=d, w=w:
                                              t.matmul(bank.ap32[:, 0:w], lh, rh,
                                                       start=(d == 0 and gi == 0),
                                                       stop=(d == c.DC - 1 and gi == 2)),
                                              reads=rds, writes=[bank],
                                              inc=(d == c.DC - 1 and gi == 2))
                            for dq in range(nh_):
                                dout = half + dq
                                bank = qbanks[dq]
                                qraw, qsf, t1, t2 = ro[5], ro[6], ro[7], ro[8]
                                tr.op("S", lambda bank=bank, dout=dout, w=w: s.activation(
                                    qraw[:, 0:w], bank.ap32[:, 0:w], AF.Identity,
                                    bias=bqt[:, dout:dout + 1]),
                                    reads=[bank, bqt], writes=[qraw])
                                tr.op("V", lambda w=w: v.stream_shuffle(
                                    qsf[:, 0:w], qraw[:, 0:w], PAIRSWAP),
                                    reads=[qraw], writes=[qsf])
                                tr.op("V", lambda w=w, slot0=slot0: v.tensor_mul(
                                    t1[:, 0:w], qraw[:, 0:w], cost[:, slot0:slot0 + w]),
                                    reads=[qraw, cost], writes=[t1])
                                tr.op("V", lambda w=w, slot0=slot0: v.tensor_mul(
                                    t2[:, 0:w], qsf[:, 0:w], ssint[:, slot0:slot0 + w]),
                                    reads=[qsf, ssint], writes=[t2])
                                tr.op("V", lambda w=w: v.tensor_add(
                                    t1[:, 0:w], t1[:, 0:w], t2[:, 0:w]),
                                    reads=[t2, t1], writes=[t1])
                                qh16, ql16 = sp16[4], sp16[5]
                                split(t1[:, 0:w], qh16[:, 0:w], ql16[:, 0:w], [t1],
                                      qh16, ql16, scale=ISCALE, eng_hi="S", eng_lo="V")
                                stc(qs_h[dout, :, tok0:tok0 + w], qh16[:, 0:w],
                                    qs_h, qh16)
                                stc(qs_l[dout, :, tok0:tok0 + w], ql16[:, 0:w],
                                    qs_l, ql16)
                tr.barrier()

            with ExitStack() as ph:
                psD = ph.enter_context(nc.psum_tensor("psD2", [128, 512], F32))[:]
                bankD = Bank(psD)
                qhh = [sb(ph, f"qhh{i}", 512, F16) for i in range(2)]
                qhl = [sb(ph, f"qhl{i}", 512, F16) for i in range(2)]
                kvh = [sb(ph, f"kvh{i}", KW, F16) for i in range(2)]
                kvl = [sb(ph, f"kvl{i}", KW, F16) for i in range(2)]
                VW = c.NKV * HD
                vch = sb(ph, "vch", 9 * VW, F16)
                vcl = sb(ph, "vcl", 9 * VW, F16)
                maskc = sb(ph, "maskc", 4 * KW, F16)
                Sm = [sb(ph, f"Sm{i}", KW) for i in range(2)]
                Aex = [sb(ph, f"Aex{i}", KW) for i in range(2)]
                An = [sb(ph, "An0", KW)] * 2
                nmax = [sb(ph, f"nmax{i}", 1) for i in range(2)]
                ssum = [sb(ph, f"ssum{i}", 1) for i in range(2)]
                rvec = [sb(ph, f"rvec{i}", 1) for i in range(2)]
                ATh = [sb(ph, f"ATh{i}", 9 * 512, F16) for i in range(2)]
                ATl = [sb(ph, f"ATl{i}", 9 * 512, F16) for i in range(2)]
                Yh = sb(ph, "Yh", c.NH * 512, F16)
                Yl = sb(ph, "Yl", c.NH * 512, F16)
                woh = [sb(ph, f"woh{i}", 512, F16) for i in range(4)]
                wol = [sb(ph, f"wol{i}", 512, F16) for i in range(4)]
                xbt = [sb(ph, f"xbt{i}", c.E) for i in range(2)]
                x1pre = [sb(ph, f"x1pre{i}", c.E) for i in range(4)]
                lnsc = sb(ph, "lnsc", c.E)
                lndump = sb(ph, "lndump", c.E, F16)
                lnv = [sb(ph, f"lnv{i}", 1) for i in range(5)]

                NQT = WIN // 128
                ATr = [ATh[i].rearrange("p (kc q) -> p kc q", kc=9) for i in range(2)]
                ATlr = [ATl[i].rearrange("p (kc q) -> p kc q", kc=9) for i in range(2)]

                def emit_scores(ch, h, qt):
                    kv = h // c.GQ
                    kb, lb = kvh[kv % 2], kvl[kv % 2]
                    qb, qlb = qhh[h % 2], qhl[h % 2]
                    sc3 = 0 if qt % 2 == 0 else 3
                    ps_sc = psA if sc3 == 0 else psB
                    sth = qb[:, qt * 128:(qt + 1) * 128]
                    stl = qlb[:, qt * 128:(qt + 1) * 128]
                    for si, (k0, kw_) in enumerate(((0, 512), (512, 512), (1024, NMETA))):
                        mvh = kb[:, k0:k0 + kw_]
                        mvl = lb[:, k0:k0 + kw_]
                        bank = banks[sc3 + si]
                        grp = [(sth, mvh, [qb, kb]), (sth, mvl, [qb, lb]),
                               (stl, mvh, [qlb, kb])]
                        for gi, (lh, rh, rds) in enumerate(grp):
                            tr.op("T", lambda lh=lh, rh=rh, k0=k0, kw_=kw_, gi=gi,
                                  ps_sc=ps_sc: t.matmul(
                                      ps_sc[:, k0:k0 + kw_], lh, rh,
                                      start=(gi == 0), stop=(gi == 2)),
                                  reads=rds, writes=[bank], inc=(gi == 2))

                def emit_softmax(ch, h, qt):
                    i2 = qt % 2
                    hp = h % 2
                    sc3 = 0 if qt % 2 == 0 else 3
                    ps_sc = psA if sc3 == 0 else psB
                    trip = [banks[sc3], banks[sc3 + 1], banks[sc3 + 2]]
                    tr.op("V", lambda: v.tensor_add(
                        Sm[i2][:, 0:KW], ps_sc[:, 0:KW],
                        maskc[:, qt * KW:(qt + 1) * KW]),
                        reads=trip + [maskc], writes=[Sm[i2]])
                    tr.op("S", lambda: s.activation(Aex[i2][:, 0:KW], Sm[i2][:, 0:KW],
                                                    AF.Exp, bias=negc,
                                                    accum_out=ssum[i2]),
                          reads=[Sm[i2], negc], writes=[Aex[i2], ssum[i2]])
                    tr.op("V", lambda: v.reciprocal(rvec[i2], ssum[i2]),
                          reads=[ssum[i2]], writes=[rvec[i2]])
                    Ann = An[i2]
                    tr.op("V", lambda Ann=Ann: v.tensor_scalar(
                        Ann[:, 0:KW], Aex[i2][:, 0:KW], rvec[i2], None, ALU.mult),
                        reads=[Aex[i2], rvec[i2]], writes=[Ann])
                    for bi in range(2):
                        bank = trip[bi]
                        for j in range(4):
                            kc = bi * 4 + j
                            tr.op("T", lambda kc=kc, j=j, bank=bank: t.transpose(
                                bank.ap32[:, j * 128:(j + 1) * 128],
                                Ann[:, kc * 128:(kc + 1) * 128], identf),
                                reads=[Ann, identf], writes=[bank], inc=(j == 3))
                        dh = ATr[hp][:, bi * 4:(bi + 1) * 4, qt * 128:(qt + 1) * 128]
                        dl = ATlr[hp][:, bi * 4:(bi + 1) * 4, qt * 128:(qt + 1) * 128]
                        br = bank.ap32.rearrange("p (j q) -> p j q", j=4)
                        tr.op("S", lambda dh=dh, br=br, bank=bank: s.activation(
                            dh, br, AF.Copy), reads=[bank], writes=[ATh[hp]])
                        tr.op("V", lambda dh=dh, dl=dl, br=br, bank=bank: v.tensor_sub(
                            dl, br, dh), reads=[bank, ATh[hp]], writes=[ATl[hp]])
                    bank = trip[2]
                    tr.op("T", lambda bank=bank: t.transpose(
                        bank.ap32[0:NMETA, 0:128], Ann[:, 1024:1024 + NMETA],
                        identf), reads=[Ann, identf], writes=[bank])
                    dh = ATr[hp][0:NMETA, 8:9, qt * 128:(qt + 1) * 128].squeeze(1)
                    dl = ATlr[hp][0:NMETA, 8:9, qt * 128:(qt + 1) * 128].squeeze(1)
                    tr.op("S", lambda dh=dh, bank=bank: s.activation(
                        dh, bank.ap32[0:NMETA, 0:128], AF.Copy),
                        reads=[bank], writes=[ATh[hp]])
                    tr.op("V", lambda dh=dh, dl=dl, bank=bank: v.tensor_sub(
                        dl, bank.ap32[0:NMETA, 0:128], dh),
                        reads=[bank, ATh[hp]], writes=[ATl[hp]])

                def emit_av(ch, h, kcs=tuple(range(9))):
                    kv = h // c.GQ
                    hp = h % 2
                    bank = banks[6] if h % 2 == 0 else bankD
                    for kc in kcs:
                        p = NMETA if kc == 8 else 128
                        sth = vch[0:p, kc * VW + kv * 128: kc * VW + (kv + 1) * 128]
                        stl = vcl[0:p, kc * VW + kv * 128: kc * VW + (kv + 1) * 128]
                        mvh = ATh[hp][0:p, kc * 512:(kc + 1) * 512]
                        mvl = ATl[hp][0:p, kc * 512:(kc + 1) * 512]
                        grp = [(sth, mvh, [vch, ATh[hp]]), (sth, mvl, [vch, ATl[hp]]),
                               (stl, mvh, [vcl, ATh[hp]])]
                        for gi, (lh, rh, rds) in enumerate(grp):
                            tr.op("T", lambda lh=lh, rh=rh, bank=bank, gi=gi, kc=kc:
                                  t.matmul(bank.ap32, lh, rh,
                                           start=(kc == 0 and gi == 0),
                                           stop=(kc == 8 and gi == 2)),
                                  reads=rds, writes=[bank],
                                  inc=(kc == 8 and gi == 2))
                    if 8 in kcs:
                        split(bank.ap32, Yh[:, h * 512:(h + 1) * 512],
                              Yl[:, h * 512:(h + 1) * 512], [bank], Yh, Yl,
                              eng_hi="S", eng_lo="V")

                def load_qh(ch, h):
                    ld(qhh[h % 2], qs_h[h, :, ch * WIN:(ch + 1) * WIN], qhh[h % 2])
                    ld(qhl[h % 2], qs_l[h, :, ch * WIN:(ch + 1) * WIN], qhl[h % 2])

                def load_kv(ch, kv):
                    ld(kvh[kv % 2][:, 0:1024],
                       ks_h[kv, :, ch * WIN: ch * WIN + 1024], kvh[kv % 2])
                    ld(kvh[kv % 2][:, 1024:KW], ks_h[kv, :, c.WKT:c.KVT],
                       kvh[kv % 2])
                    ld(kvl[kv % 2][:, 0:1024],
                       ks_l[kv, :, ch * WIN: ch * WIN + 1024], kvl[kv % 2])
                    ld(kvl[kv % 2][:, 1024:KW], ks_l[kv, :, c.WKT:c.KVT],
                       kvl[kv % 2])

                def load_vmask(ch):
                    for bi in range(9):
                        blk = c.NVB if bi == 8 else ch * 4 + bi
                        p = NMETA if bi == 8 else 128
                        ld(vch[0:p, bi * VW:(bi + 1) * VW], vs_h[blk, 0:p, :], vch)
                        ld(vcl[0:p, bi * VW:(bi + 1) * VW], vs_l[blk, 0:p, :], vcl)
                    ld(maskc.rearrange("p (a k) -> p a k", a=4),
                       mask_in[ch].rearrange("(a p) k -> p a k", p=128), maskc)

                load_vmask(0)
                load_qh(0, 0)
                load_kv(0, 0)
                for ch in range(c.NCH):
                    for h in range(c.NH):
                        if h + 1 < c.NH:
                            load_qh(ch, h + 1)
                        elif ch + 1 < c.NCH:
                            load_qh(ch + 1, 0)
                        if h % c.GQ == 0 and (h // c.GQ) + 1 < c.NKV:
                            load_kv(ch, h // c.GQ + 1)
                        elif (h % c.GQ == 0 and (h // c.GQ) + 1 == c.NKV
                              and ch + 1 < c.NCH and c.NKV % 2 == 0):
                            load_kv(ch + 1, 0)
                        emit_scores(ch, h, 0)
                        emit_scores(ch, h, 1)
                        if h > 0:
                            emit_av(ch, h - 1, tuple(range(0, 4)))
                        emit_softmax(ch, h, 0)
                        if h > 0:
                            emit_av(ch, h - 1, tuple(range(4, 9)))
                        emit_scores(ch, h, 2)
                        emit_softmax(ch, h, 1)
                        emit_scores(ch, h, 3)
                        emit_softmax(ch, h, 2)
                        emit_softmax(ch, h, 3)
                    emit_av(ch, c.NH - 1)
                    if ch + 1 < c.NCH:
                        load_vmask(ch + 1)
                        if c.NKV % 2 == 1:
                            load_kv(ch + 1, 0)

                    nsl = c.E // 512
                    obanks = [banks[0], banks[1], banks[2], banks[3]]
                    for sl in range(nsl):
                        for d in range(c.DC):
                            wb = d % 4
                            ld(woh[wb], wos_h[d, :, sl * 512:(sl + 1) * 512], woh[wb])
                            ld(wol[wb], wos_l[d, :, sl * 512:(sl + 1) * 512], wol[wb])
                            for tsub in range(NQT):
                                sth = Yh[:, d * 512 + tsub * 128:
                                         d * 512 + (tsub + 1) * 128]
                                stl = Yl[:, d * 512 + tsub * 128:
                                         d * 512 + (tsub + 1) * 128]
                                bank = obanks[tsub]
                                grp = [(sth, woh[wb], [Yh, woh[wb]]),
                                       (sth, wol[wb], [Yh, wol[wb]]),
                                       (stl, woh[wb], [Yl, woh[wb]])]
                                for gi, (lh, rh, rds) in enumerate(grp):
                                    tr.op("T", lambda lh=lh, rh=rh, gi=gi, d=d,
                                          bank=bank: t.matmul(
                                              bank.ap32, lh, rh,
                                              start=(d == 0 and gi == 0),
                                              stop=(d == c.DC - 1 and gi == 2)),
                                          reads=rds, writes=[bank],
                                          inc=(d == c.DC - 1 and gi == 2))
                        for tsub in range(NQT):
                            bank = obanks[tsub]
                            tr.op("S", lambda sl=sl, tsub=tsub, bank=bank: s.activation(
                                x1pre[tsub][:, sl * 512:(sl + 1) * 512],
                                bank.ap32, AF.Copy),
                                reads=[bank], writes=[x1pre[tsub]])
                    for tsub in range(NQT):
                        xb_ = xbt[tsub % 2]
                        r0row = WIN + ch * WIN + tsub * 128
                        ld(xb_, xk[r0row:r0row + 128, :], xb_)
                        tr.op("V", lambda xb_=xb_: v.tensor_add(xb_, xb_, bob),
                              reads=[bob, xb_], writes=[xb_])
                        tr.op("V", lambda tsub=tsub, xb_=xb_: v.tensor_add(
                            x1pre[tsub], x1pre[tsub], xb_),
                            reads=[xb_, x1pre[tsub]], writes=[x1pre[tsub]])
                        emit_ln(tr, nc, x1pre[tsub], x1pre[tsub], lnsc, lndump, lnv,
                                c.E, [x1pre[tsub]], [x1pre[tsub]])
                        stc(x1s[ch * WIN + tsub * 128: ch * WIN + (tsub + 1) * 128, :],
                            x1pre[tsub], x1s, x1pre[tsub])
                tr.barrier()

            with ExitStack() as ph:
                psE = ph.enter_context(nc.psum_tensor("psE", [128, 1024], F16))[:]
                bankE = Bank(psE)
                wi32 = [sb(ph, f"wi32_{i}", c.E, I32) for i in range(2)]
                wc16 = [sb(ph, f"wc16_{i}", c.E, F16) for i in range(2)]
                wt16 = [sb(ph, f"wt16_{i}", c.E, F16) for i in range(2)]
                for ic in range(c.IC):
                    for wsel in range(2):
                        bsel = (ic * 2 + wsel) % 2
                        wi, wc, wt = wi32[bsel], wc16[bsel], wt16[bsel]
                        src_rows = gw if wsel == 0 else uw
                        ld(wi, src_rows[ic * 128:(ic + 1) * 128, :], wi)
                        tr.op("V", lambda wi=wi, wc=wc: v.tensor_scalar(
                            wc, wi, -128, None, ALU.add), reads=[wi], writes=[wc])
                        for hcg in range(0, c.DC, 8):
                            nn = min(8, c.DC - hcg)
                            for q in range(nn):
                                tr.op("T", lambda wc=wc, hcg=hcg, q=q: t.transpose(
                                    psE[:, q * 128:(q + 1) * 128],
                                    wc[:, (hcg + q) * 128:(hcg + q + 1) * 128],
                                    identh),
                                    reads=[wc, identh], writes=[bankE],
                                    inc=(q == nn - 1))
                            tr.op("S", lambda wt=wt, hcg=hcg, nn=nn: s.copy(
                                wt[:, hcg * 128:(hcg + nn) * 128], psE[:, 0:nn * 128]),
                                reads=[bankE], writes=[wt])
                        stc((wgs if wsel == 0 else wus)[ic], wt,
                            wgs if wsel == 0 else wus, wt)
                DHALF = c.I // 2
                dwi = [sb(ph, f"dwi{i}", DHALF, I32) for i in range(2)]
                dwc = [sb(ph, f"dwc{i}", DHALF, F16) for i in range(2)]
                dwt = [sb(ph, f"dwt{i}", DHALF, F16) for i in range(2)]
                for hc in range(c.DC):
                    for half in range(2):
                        bsel = (hc * 2 + half) % 2
                        wi, wc, wt = dwi[bsel], dwc[bsel], dwt[bsel]
                        ld(wi, dw[hc * 128:(hc + 1) * 128,
                                  half * DHALF:(half + 1) * DHALF], wi)
                        tr.op("V", lambda wi=wi, wc=wc: v.tensor_scalar(
                            wc, wi, -128, None, ALU.add), reads=[wi], writes=[wc])
                        nic = DHALF // 128
                        for icg in range(0, nic, 8):
                            nn = min(8, nic - icg)
                            for q in range(nn):
                                tr.op("T", lambda wc=wc, icg=icg, q=q: t.transpose(
                                    psE[:, q * 128:(q + 1) * 128],
                                    wc[:, (icg + q) * 128:(icg + q + 1) * 128],
                                    identh),
                                    reads=[wc, identh], writes=[bankE],
                                    inc=(q == nn - 1))
                            tr.op("S", lambda wt=wt, icg=icg, nn=nn: s.copy(
                                wt[:, icg * 128:(icg + nn) * 128], psE[:, 0:nn * 128]),
                                reads=[bankE], writes=[wt])
                        ic0 = half * DHALF // 128
                        if nic >= c.IGRP:
                            g0 = ic0 // c.IGRP
                            ng = nic // c.IGRP
                            dst = wds[g0:g0 + ng, hc].rearrange("g p x -> p g x")
                            srcv = wt[:, 0:nic * 128].rearrange(
                                "p (g x) -> p g x", g=ng)
                            stc(dst, srcv, wds, wt)
                        else:
                            g0, i0 = ic0 // c.IGRP, ic0 % c.IGRP
                            stc(wds[g0, hc, :, i0 * 128:(i0 + nic) * 128],
                                wt[:, 0:nic * 128], wds, wt)
            tr.barrier()

            with ExitStack() as ph:
                psD = ph.enter_context(nc.psum_tensor("psD4", [128, 512], F32))[:]
                bankD = Bank(psD)
                x1tm = [sb(ph, f"x1tm{i}", c.E) for i in range(2)]
                x1Th = sb(ph, "x1Th", c.DC * 512, F16)
                x1Tl = sb(ph, "x1Tl", c.DC * 512, F16)
                x2acc = sb(ph, "x2acc", c.DC * 512)
                hacth = [sb(ph, f"hacth{i}", c.IGRP * 512, F16) for i in range(2)]
                hactl = [sb(ph, f"hactl{i}", c.IGRP * 512, F16) for i in range(2)]
                wgT = [sb(ph, f"wgT{i}", c.E, F16) for i in range(2)]
                wuT = [sb(ph, f"wuT{i}", c.E, F16) for i in range(2)]
                wdTt = [sb(ph, f"wdTt{i}", c.IGRP * 128, F16) for i in range(2)]
                silu_t = [sb(ph, f"silu{i}", 512) for i in range(2)]
                h32 = [sb(ph, f"h32_{i}", 512) for i in range(2)]
                outt = [sb(ph, f"outt{i}", c.E) for i in range(2)]
                lnsc = sb(ph, "lnsc4", c.E)
                lndump = sb(ph, "lndump4", c.E)
                lnv = [sb(ph, f"lnv4{i}", 1) for i in range(5)]

                TB = 512
                NTB = c.T // TB

                def emit_gu(tb, grp):
                    hh, hl = hacth[grp % 2], hactl[grp % 2]
                    for ii in range(c.IGRP):
                        ic = grp * c.IGRP + ii
                        wb = ic % 2
                        ld(wgT[wb], wgs[ic], wgT[wb])
                        ld(wuT[wb], wus[ic], wuT[wb])
                        for which, (bank, wT) in enumerate(
                                ((banks[6], wgT[wb]), (bankD, wuT[wb]))):
                            for hc in range(c.DC):
                                for gi, (x16, xbuf) in enumerate(
                                        ((x1Th, x1Th), (x1Tl, x1Tl))):
                                    tr.op("T", lambda wT=wT, x16=x16, hc=hc, gi=gi,
                                          bank=bank: t.matmul(
                                              bank.ap32,
                                              wT[:, hc * 128:(hc + 1) * 128],
                                              x16[:, hc * 512:(hc + 1) * 512],
                                              start=(hc == 0 and gi == 0),
                                              stop=(hc == c.DC - 1 and gi == 1)),
                                          reads=[wT, xbuf], writes=[bank],
                                          inc=(hc == c.DC - 1 and gi == 1))
                        sl_, hv = silu_t[ii % 2], h32[ii % 2]
                        tr.op("S", lambda sl_=sl_, ic=ic: s.activation(
                            sl_, banks[6].ap32, AF.Silu, scale=gsc[:, ic:ic + 1]),
                            reads=[banks[6], gsc], writes=[sl_])
                        tr.op("V", lambda hv=hv, sl_=sl_, ic=ic: v.scalar_tensor_tensor(
                            hv, psD, usc[:, ic:ic + 1], sl_, ALU.mult, ALU.mult),
                            reads=[bankD, usc, sl_], writes=[hv])
                        split(hv, hh[:, ii * 512:(ii + 1) * 512],
                              hl[:, ii * 512:(ii + 1) * 512], [hv], hh, hl, scale=HSC)

                def emit_down(tb, grp):
                    hh, hl = hacth[grp % 2], hactl[grp % 2]
                    for ho in range(c.DC):
                        wb = ho % 2
                        ld(wdTt[wb], wds[grp, ho], wdTt[wb])
                        bank = banks[ho % 4]
                        for ii in range(c.IGRP):
                            for gi, (h16, hbuf) in enumerate(((hh, hh), (hl, hl))):
                                tr.op("T", lambda h16=h16, ii=ii, wb=wb, gi=gi,
                                      bank=bank: t.matmul(
                                          bank.ap32,
                                          wdTt[wb][:, ii * 128:(ii + 1) * 128],
                                          h16[:, ii * 512:(ii + 1) * 512],
                                          start=(ii == 0 and gi == 0),
                                          stop=(ii == c.IGRP - 1 and gi == 1)),
                                      reads=[wdTt[wb], hbuf], writes=[bank],
                                      inc=(ii == c.IGRP - 1 and gi == 1))
                        if grp == 0:
                            tr.op("S", lambda ho=ho, bank=bank: s.activation(
                                x2acc[:, ho * 512:(ho + 1) * 512], bank.ap32,
                                AF.Copy, scale=dsc8[:, ho:ho + 1]),
                                reads=[bank, dsc8], writes=[x2acc])
                        else:
                            tr.op("V", lambda ho=ho, bank=bank: v.scalar_tensor_tensor(
                                x2acc[:, ho * 512:(ho + 1) * 512], bank.ap32,
                                dsc8[:, ho:ho + 1], x2acc[:, ho * 512:(ho + 1) * 512],
                                ALU.mult, ALU.add),
                                reads=[bank, dsc8, x2acc], writes=[x2acc])

                for tb in range(NTB):
                    for j in range(TB // 128):
                        xb_ = x1tm[j % 2]
                        ld(xb_, x1s[tb * TB + j * 128: tb * TB + (j + 1) * 128, :], xb_)
                        blocks = []
                        for dc in range(c.DC):
                            def dfn(buf16, dc=dc, j=j):
                                return buf16[:, dc * 512 + j * 128: dc * 512 + (j + 1) * 128]
                            blocks.append((dc * 128, dfn))
                        tp_split(xb_, xb_, blocks, x1Th, x1Tl, x1Th, x1Tl,
                                 bank_pick=lambda i: [banks[0], banks[1]][i])
                    for grp in range(c.NGRP):
                        emit_gu(tb, grp)
                        if grp > 0:
                            emit_down(tb, grp - 1)
                    emit_down(tb, c.NGRP - 1)

                    for tsub in range(TB // 128):
                        xr = x1tm[tsub % 2]
                        ld(xr, x1s[tb * TB + tsub * 128: tb * TB + (tsub + 1) * 128, :],
                           xr)
                        ot = outt[tsub % 2]
                        for hcg in range(0, c.DC, 4):
                            nn = min(4, c.DC - hcg)
                            bank = banks[(hcg // 4) % 2]
                            for q in range(nn):
                                tr.op("T", lambda hcg=hcg, q=q, tsub=tsub, bank=bank:
                                      t.transpose(bank.ap32[:, q * 128:(q + 1) * 128],
                                                  x2acc[:, (hcg + q) * 512 + tsub * 128:
                                                        (hcg + q) * 512 + (tsub + 1) * 128],
                                                  identf),
                                      reads=[x2acc, identf], writes=[bank],
                                      inc=(q == nn - 1))
                            tr.op("V", lambda hcg=hcg, nn=nn, ot=ot, xr=xr, bank=bank:
                                  v.tensor_add(ot[:, hcg * 128:(hcg + nn) * 128],
                                               bank.ap32[:, 0:nn * 128],
                                               xr[:, hcg * 128:(hcg + nn) * 128]),
                                  reads=[bank, xr], writes=[ot])
                        emit_ln(tr, nc, ot, ot, lnsc, lndump, lnv, c.E, [ot], [ot])
                        stc(y[tb * TB + tsub * 128: tb * TB + (tsub + 1) * 128, :],
                            ot, y, ot)
            tr.finish("Y")
    return nc


def emit_ln(tr, nc, src, dst, lnsc, lndump, lnv, E, src_bufs, dst_bufs):
    v, s = nc.vector, nc.scalar
    mean, ssq, varep, sd, rs = lnv
    tr.op("V", lambda: v.reduce_sum(mean, src, axis=AX.X), reads=src_bufs,
          writes=[mean])
    tr.op("V", lambda: v.tensor_scalar(mean, mean, 1.0 / E, None, ALU.mult),
          reads=[mean], writes=[mean])
    tr.op("V", lambda: v.tensor_scalar(lnsc, src, mean, None, ALU.subtract),
          reads=src_bufs + [mean], writes=[lnsc])
    tr.op("S", lambda: s.activation(lndump, lnsc, AF.Square, accum_out=ssq),
          reads=[lnsc], writes=[lndump, ssq])
    tr.op("V", lambda: v.tensor_scalar(varep, ssq, 1.0 / E, EPS, ALU.mult, ALU.add),
          reads=[ssq], writes=[varep])
    tr.op("S", lambda: s.activation(sd, varep, AF.Sqrt), reads=[varep], writes=[sd])
    tr.op("V", lambda: v.reciprocal(rs, sd), reads=[sd], writes=[rs])
    tr.op("V", lambda: v.tensor_scalar(dst, lnsc, rs, None, ALU.mult),
          reads=[lnsc, rs], writes=dst_bufs)



def make_masks(cfg, r0):
    c = cfg
    out = np.full((c.NCH, WIN, KW), NEG, dtype=np.float64)
    for chl in range(c.NCH):
        qsg = r0 + chl * WIN
        ls = max(NMETA, qsg - WIN + 1)
        q_pos = qsg + np.arange(WIN)[:, None]
        k_win = qsg - WIN + np.arange(2 * WIN)[None, :]
        member = k_win >= ls
        causal = k_win <= q_pos
        in_win = k_win >= q_pos - WIN + 1
        meta_f = k_win < NMETA
        allowed_w = member & causal & (in_win | meta_f) & (k_win >= 0)
        k_meta = np.arange(NMETA)[None, :]
        allowed_m = k_meta <= q_pos
        wpart = out[chl, :, 0:2 * WIN]
        wpart[allowed_w] = 0.0
        mpart = out[chl, :, 2 * WIN:]
        mpart[np.broadcast_to(allowed_m, mpart.shape)] = 0.0
    return out.astype(np.float16)


def _inv_freq_f32():
    y32 = (np.arange(0, HD, 2).astype(np.float32) / np.float32(HD))
    try:
        import ctypes
        libm = ctypes.CDLL("libm.so.6")
        libm.powf.restype = ctypes.c_float
        libm.powf.argtypes = [ctypes.c_float, ctypes.c_float]
        p = np.array([libm.powf(ctypes.c_float(10000.0), ctypes.c_float(v))
                      for v in y32], dtype=np.float32)
    except Exception:
        p = (np.float32(10000.0) ** y32).astype(np.float32)
    return (np.float32(1.0) / p).astype(np.float32)


_ROPE_CACHE = {}


def _rope_all(cfg):
    key = (cfg.S,)
    if key in _ROPE_CACHE:
        return _ROPE_CACHE[key]
    inv = _inv_freq_f32()
    pos = np.arange(-WIN, cfg.S).astype(np.float32)
    ang = (pos[:, None] * inv[None, :]).astype(np.float32)
    try:
        import ctypes
        libm = ctypes.CDLL("libm.so.6")
        libm.cosf.restype = ctypes.c_float
        libm.cosf.argtypes = [ctypes.c_float]
        libm.sinf.restype = ctypes.c_float
        libm.sinf.argtypes = [ctypes.c_float]
        flat = ang.ravel()
        cf, sf = libm.cosf, libm.sinf
        cfl = ctypes.c_float
        cos_t = np.fromiter((cf(cfl(a)) for a in flat), np.float32,
                            count=flat.size).reshape(ang.shape)
        sin_t = np.fromiter((sf(cfl(a)) for a in flat), np.float32,
                            count=flat.size).reshape(ang.shape)
    except Exception:
        cos_t = np.cos(ang.astype(np.float64)).astype(np.float32)
        sin_t = np.sin(ang.astype(np.float64)).astype(np.float32)
    _ROPE_CACHE[key] = (cos_t, sin_t)
    return cos_t, sin_t


def make_rope(cfg, r0):
    c = cfg
    cos_all, sin_all = _rope_all(c)
    pos_idx = np.concatenate([r0 - WIN + np.arange(c.WKT),
                              np.arange(NMETA)]) + WIN
    j = np.arange(HD)
    cos_t = cos_all[pos_idx][:, j // 2].T.astype(np.float64)
    sin_t = sin_all[pos_idx][:, j // 2].T.astype(np.float64)
    sign = np.where(j % 2 == 0, -1.0, 1.0)
    sin_t = sin_t * sign[:, None]
    return (np.ascontiguousarray(cos_t.astype(np.float32)),
            np.ascontiguousarray(sin_t.astype(np.float32)))


def make_core_inputs(cfg, core, x, Wq, bq, Wkv, bkv, Wo, bo,
                     gwv, gscv, uwv, uscv, dwv, dscv):
    c = cfg
    per_b = c.n_cores // c.B
    b = core // per_b
    r0 = (core % per_b) * c.T
    lo = r0 - WIN
    xw = np.zeros((c.WKT, c.E), dtype=np.float32)
    src0 = max(0, lo)
    xw[src0 - lo:, :] = x[b, src0: r0 + c.T, :]
    xkc = np.concatenate([xw, x[b, 0:NMETA, :]], axis=0)
    cos_t, sin_t = make_rope(c, r0)
    return {
        "xk": np.ascontiguousarray(xkc),
        "wq": Wq, "wkv": Wkv, "wo": Wo,
        "gw": gwv, "uw": uwv, "dw": dwv,
        "bq_t": np.ascontiguousarray(bq.reshape(c.DC, 128).T),
        "bkv_k": np.ascontiguousarray(bkv[0:c.NKV * HD].reshape(c.NKV, 128).T),
        "bkv_vb": np.ascontiguousarray(
            np.broadcast_to(bkv[c.NKV * HD:], (128, c.NKV * HD))),
        "bo_b": np.ascontiguousarray(np.broadcast_to(bo, (128, c.E))),
        "gsc_t": np.ascontiguousarray(gscv.reshape(c.IC, 128).T),
        "usc_t": np.ascontiguousarray(uscv.reshape(c.IC, 128).T),
        "dsc_t": np.ascontiguousarray(dscv.reshape(c.DC, 128).T),
        "cos_t": cos_t, "sin_t": sin_t,
        "mask_t": make_masks(c, r0),
        "identf": np.eye(128, dtype=np.float32),
        "identh": np.eye(128, dtype=np.float16),
    }


def kernel(L0_attn_x, L0_attn_Wq, L0_attn_bq, L0_attn_Wkv, L0_attn_bkv,
           L0_attn_Wo, L0_attn_bo, L0_ffn_gate_w, L0_ffn_gate_scale,
           L0_ffn_up_w, L0_ffn_up_scale, L0_ffn_down_w, L0_ffn_down_scale):
    cfg = FULL
    nc = build(cfg)
    x = np.asarray(L0_attn_x, dtype=np.float32)
    in_maps = [make_core_inputs(
        cfg, core, x,
        np.asarray(L0_attn_Wq, dtype=np.float32),
        np.asarray(L0_attn_bq, dtype=np.float32),
        np.asarray(L0_attn_Wkv, dtype=np.float32),
        np.asarray(L0_attn_bkv, dtype=np.float32),
        np.asarray(L0_attn_Wo, dtype=np.float32),
        np.asarray(L0_attn_bo, dtype=np.float32),
        np.asarray(L0_ffn_gate_w, dtype=np.int32),
        np.asarray(L0_ffn_gate_scale, dtype=np.float32),
        np.asarray(L0_ffn_up_w, dtype=np.int32),
        np.asarray(L0_ffn_up_scale, dtype=np.float32),
        np.asarray(L0_ffn_down_w, dtype=np.int32),
        np.asarray(L0_ffn_down_scale, dtype=np.float32))
        for core in range(cfg.n_cores)]
    res = run_bass_kernel_spmd(nc, in_maps, core_ids=list(range(cfg.n_cores)))
    out = np.empty((cfg.B, cfg.S, cfg.E), dtype=np.float32)
    per_b = cfg.n_cores // cfg.B
    for core in range(cfg.n_cores):
        b = core // per_b
        r0 = (core % per_b) * cfg.T
        out[b, r0:r0 + cfg.T, :] = res.results[core]["y"]
    return out
